# version 01
# baseline (speedup 1.0000x reference)
import contextlib
import numpy as np
import ml_dtypes
import concourse.bass as bass
import concourse.mybir as mybir
from concourse.bass_utils import run_bass_kernel_spmd

F32 = mybir.dt.float32
BF16 = mybir.dt.bfloat16
AFT = mybir.ActivationFunctionType
ALU = mybir.AluOpType
AX = mybir.AxisListType

B, N, D = 4, 4096, 2048
H, KV, HD = 16, 4, 128
NB = N // 128
NOWN = NB // 2
FH = 8192
ROPE_THETA = 10000.0
GATE_NORM = 16.0
SCALE = HD ** (-0.5)
EPS1 = 1e-5
EPS2 = 1e-8

AF32_COLS = 16384
AB16_COLS = 49152

ENGINES = ("tensor", "vector", "scalar", "sync", "gpsimd")


class Op:
    __slots__ = ("oid", "engine", "fn", "deps", "inc", "seq", "is_dma")

    def __init__(self, oid, engine, fn, deps, is_dma):
        self.oid = oid
        self.engine = engine
        self.fn = fn
        self.deps = deps
        self.inc = False
        self.seq = 0
        self.is_dma = is_dma


class Buf:
    __slots__ = ("ap", "writer", "readers", "name")

    def __init__(self, ap, name=""):
        self.ap = ap
        self.writer = None
        self.readers = []
        self.name = name

    def __getitem__(self, sl):
        return self.ap[sl]


class Prog:
    def __init__(self):
        self.ops = []
        self.fence_deps = ()
        self.last_op = {}

    def fence(self):
        last = {}
        for o in self.ops:
            last[o.engine] = o.oid
        self.fence_deps = tuple(last.values())

    def op(self, engine, fn, reads=(), writes=(), extra_deps=(), is_dma=False):
        deps = set(extra_deps)
        deps.update(self.fence_deps)
        for b in reads:
            if b.writer is not None:
                deps.add(b.writer)
        for b in writes:
            if b.writer is not None:
                deps.add(b.writer)
            deps.update(b.readers)
        oid = len(self.ops)
        o = Op(oid, engine, fn, deps, is_dma)
        self.ops.append(o)
        self.last_op[engine] = oid
        for b in reads:
            b.readers.append(oid)
        for b in writes:
            b.writer = oid
            b.readers = []
        return oid

    def dma(self, engine, out_ap, in_ap, reads=(), writes=(), extra_deps=(),
            slow=False):
        return self.op(engine,
                       lambda e, o=out_ap, i=in_ap, sl=slow: e.dma_start(
                           out=o, in_=i, allow_slow_non_contiguous=sl),
                       reads=reads, writes=writes, extra_deps=extra_deps,
                       is_dma=True)

    def emit(self, nc, es, block):
        ops = self.ops
        by_engine = {e: [o for o in ops if o.engine == e] for e in ENGINES}
        for e in ENGINES:
            for i, o in enumerate(by_engine[e]):
                o.seq = i
        WIN = 8
        for o in ops:
            kept = set()
            for d in o.deps:
                if ops[d].engine != o.engine:
                    kept.add(d)
                elif ops[d].is_dma or o.is_dma:
                    kept.add(d)
                elif o.seq - ops[d].seq <= WIN:
                    kept.add(d)
            o.deps = kept
            for d in o.deps:
                ops[d].inc = True
            if o.is_dma:
                o.inc = True
        sems = {e: es.enter_context(nc.semaphore(f"sem_{e}")) for e in ENGINES}
        cnt = {e: 0 for e in ENGINES}
        post = {}
        for e in ENGINES:
            c = 0
            for o in by_engine[e]:
                if o.inc:
                    c += 16 if o.is_dma else 1
                post[o.oid] = c
        for e in ENGINES:
            seen = {e2: 0 for e2 in ENGINES}
            for o in by_engine[e]:
                ws = {}
                for d in o.deps:
                    pe = ops[d].engine
                    v = post[d]
                    if v > seen[pe]:
                        ws[pe] = max(ws.get(pe, 0), v)
                for pe, v in ws.items():
                    seen[pe] = v
                o.deps = sorted(ws.items())
        for ename in ENGINES:
            eops = by_engine[ename]
            if not eops:
                continue
            def mkbody(eops_):
                def body(eng):
                    for o in eops_:
                        for pe, v in o.deps:
                            eng.wait_ge(sems[pe], v)
                        inst = o.fn(eng)
                        if o.inc:
                            if inst is None:
                                inst = eng.nop()
                            inst.then_inc(sems[ename], 16 if o.is_dma else 1)
                return body
            getattr(block, ename)(mkbody(list(eops)))


def _host_constants():
    c = {}
    c["ident_f32"] = np.eye(128, dtype=np.float32)
    c["ident_bf16"] = np.eye(128, dtype=ml_dtypes.bfloat16)
    c["lmat"] = np.triu(np.full((128, 128), -1.0 / GATE_NORM)).astype(
        ml_dtypes.bfloat16)
    c["ones_col_bf16"] = np.ones((128, 1), dtype=ml_dtypes.bfloat16)
    c["ones_row_f32"] = np.ones((1, 128), dtype=np.float32)
    tril = (np.arange(128)[None, :] >= np.arange(128)[:, None]).astype(np.float32)
    c["_tril4"] = np.ascontiguousarray(np.tile(tril, (1, 4)))
    pos = np.arange(N, dtype=np.float64)[:, None]
    inv = 1.0 / (ROPE_THETA ** (np.arange(0, HD, 2, dtype=np.float64) / HD))
    fr = pos * inv[None, :]
    for nm, tab in (("cosk", np.cos(fr)), ("sink", np.sin(fr)),
                    ("cosq", np.cos(fr) * SCALE), ("sinq", np.sin(fr) * SCALE)):
        hi = tab.astype(np.float32)
        lo = (tab - hi.astype(np.float64)).astype(np.float32)
        c[nm + "_hi"] = np.ascontiguousarray(hi)
        c[nm + "_lo"] = np.ascontiguousarray(lo)
    return c


def _parity_inputs(par, consts):
    d = {}
    ones = np.ones((128, 512), dtype=np.float32)
    zeros = np.zeros((128, 512), dtype=np.float32)
    tril4 = consts["_tril4"]
    d["mdiag0"] = tril4 if par == 0 else ones
    d["mdiag1"] = zeros if par == 0 else tril4
    d["mgla0"] = tril4 if par == 0 else zeros
    d["mgla1"] = zeros if par == 0 else tril4
    w = np.zeros((128, 2), dtype=np.float32)
    w[:, par] = 1.0
    d["wsel"] = w
    idxs = ((2 * np.arange(NOWN)[:, None] + par) * 128
            + np.arange(128)[None, :]).ravel()
    for nm in ("cosq_hi", "cosq_lo", "sinq_hi", "sinq_lo"):
        d["o" + nm] = np.ascontiguousarray(consts[nm][idxs])
    d["_idxs"] = idxs
    return d


def build_program(dbg=()):
    nc = bass.Bass("TRN2", target_bir_lowering=False)

    def di(name, shape, dt):
        return nc.dram_tensor(name, list(shape), dt, kind="ExternalInput").ap()

    def drm(name, shape, dt):
        kind = "ExternalOutput" if name in dbg else "Internal"
        return nc.dram_tensor(name, list(shape), dt, kind=kind).ap()

    xb = di("xb", (N, D), F32)
    xown = di("xown", (NOWN * 128, D), F32)
    wq = di("wq", (D, D), F32)
    wk = di("wk", (D, KV * HD), F32)
    wv = di("wv", (D, KV * HD), F32)
    wo = di("wo", (D, D), F32)
    wg = di("wg", (FH, D), F32)
    wu = di("wu", (FH, D), F32)
    wd = di("wd", (D, FH), F32)
    ident_f32_d = di("ident_f32", (128, 128), F32)
    ident_bf16_d = di("ident_bf16", (128, 128), BF16)
    lmat_d = di("lmat", (128, 128), BF16)
    onesc_d = di("ones_col_bf16", (128, 1), BF16)
    onesr_d = di("ones_row_f32", (1, 128), F32)
    mdiag0_d = di("mdiag0", (128, 512), F32)
    mdiag1_d = di("mdiag1", (128, 512), F32)
    mgla0_d = di("mgla0", (128, 512), F32)
    mgla1_d = di("mgla1", (128, 512), F32)
    wsel_d = di("wsel", (128, 2), F32)
    ropes = {nm: di(nm, (N, 64), F32)
             for nm in ("cosk_hi", "cosk_lo", "sink_hi", "sink_lo")}
    oropes = {nm: di(nm, (NOWN * 128, 64), F32)
              for nm in ("ocosq_hi", "ocosq_lo", "osinq_hi", "osinq_lo")}

    out = nc.dram_tensor("out", [NOWN * 128, D], F32, kind="ExternalOutput").ap()

    skT_hi = drm("skT_hi", (KV * HD, N), BF16)
    skT_lo = drm("skT_lo", (KV * HD, N), BF16)
    ktT_hi = drm("ktT_hi", (KV * HD, N), BF16)
    ktT_lo = drm("ktT_lo", (KV * HD, N), BF16)
    khat_hi = drm("khat_hi", (N, KV * HD), BF16)
    khat_lo = drm("khat_lo", (N, KV * HD), BF16)
    svs_hi = drm("svs_hi", (N, KV * HD), BF16)
    svs_lo = drm("svs_lo", (N, KV * HD), BF16)
    expLam = drm("expLam", (N, KV * HD), F32)
    dCT = drm("dCT", (128, NB * KV), F32)
    sqT_hi = drm("sqT_hi", (128, NOWN * H * 128), BF16)
    sqT_lo = drm("sqT_lo", (128, NOWN * H * 128), BF16)
    qtT_hi = drm("qtT_hi", (128, NOWN * H * 128), BF16)
    qtT_lo = drm("qtT_lo", (128, NOWN * H * 128), BF16)
    oT_hi = drm("oT_hi", (128, H * NOWN * 128), BF16)
    oT_lo = drm("oT_lo", (128, H * NOWN * 128), BF16)
    woh = drm("woh", (D, D), BF16)
    wol = drm("wol", (D, D), BF16)
    wgT = drm("wgT", (FH // 128 * 16 * 128, 128), BF16)
    wuT = drm("wuT", (FH // 128 * 16 * 128, 128), BF16)
    wdT = drm("wdT", (16 * 64 * 128, 128), BF16)
    x1s = drm("x1s", (NOWN * 128, D), F32)
    dbg_xn = drm("dbg_xn", (N, D), F32)
    dbg_xnT_hi = drm("dbg_xnT_hi", (N, 2048), BF16)
    dbg_xnT_lo = drm("dbg_xnT_lo", (N, 2048), BF16)
    dbg_k = drm("dbg_k", (N, 512), F32)
    dbg_ogla = drm("dbg_ogla", (KV * 128, 16 * 512), F32)
    dbg_l = drm("dbg_l", (KV * NOWN, 512), F32)
    dbg_gb = drm("dbg_gb", (NOWN * 128, D), F32)
    dbg_w = drm("dbg_w", (NOWN * 128, D), F32)
    xqT = drm("xqT", (16 * 128, NOWN * 128), BF16)
    sinvs = drm("sinvs", (NOWN * 128, 1), F32)

    prog = Prog()
    es = contextlib.ExitStack()

    with es:
        AFA = es.enter_context(nc.sbuf_tensor("af32", [128, AF32_COLS], F32))
        ABA = es.enter_context(nc.sbuf_tensor("ab16", [128, AB16_COLS], BF16))
        statT = es.enter_context(nc.sbuf_tensor("stat", [128, 64], F32))
        dct_s = es.enter_context(nc.sbuf_tensor("dct_s", [128, 128], F32))
        idf_s = es.enter_context(nc.sbuf_tensor("idf_s", [128, 128], F32))
        idb_s = es.enter_context(nc.sbuf_tensor("idb_s", [128, 128], BF16))
        lm_s = es.enter_context(nc.sbuf_tensor("lm_s", [128, 128], BF16))
        oc_s = es.enter_context(nc.sbuf_tensor("oc_s", [128, 1], BF16))
        or_s = es.enter_context(nc.sbuf_tensor("or_s", [1, 128], F32))
        md0_s = es.enter_context(nc.sbuf_tensor("md0_s", [128, 512], F32))
        md1_s = es.enter_context(nc.sbuf_tensor("md1_s", [128, 512], F32))
        mg0_s = es.enter_context(nc.sbuf_tensor("mg0_s", [128, 512], F32))
        mg1_s = es.enter_context(nc.sbuf_tensor("mg1_s", [128, 512], F32))
        wsel_s = es.enter_context(nc.sbuf_tensor("wsel_s", [128, 2], F32))
        rope_s = es.enter_context(nc.sbuf_tensor("rope_s", [128, 512], F32))
        row_s = es.enter_context(nc.sbuf_tensor("row_s", [1, 512], F32))
        pbt = [es.enter_context(nc.psum_tensor(f"pb{i}", [128, 512], F32))
               for i in range(8)]

        stat = Buf(statT[:, :], "stat")
        dct = Buf(dct_s[:, :], "dct")
        idf = Buf(idf_s[:, :], "idf")
        idb = Buf(idb_s[:, :], "idb")
        lm = Buf(lm_s[:, :], "lm")
        oc = Buf(oc_s[:, :], "oc")
        orow = Buf(or_s[:, :], "orow")
        md0 = Buf(md0_s[:, :], "md0")
        md1 = Buf(md1_s[:, :], "md1")
        mg0 = Buf(mg0_s[:, :], "mg0")
        mg1 = Buf(mg1_s[:, :], "mg1")
        wsel = Buf(wsel_s[:, :], "wsel")
        ropeb = Buf(rope_s[:, :], "ropeb")
        rowb = Buf(row_s[:, :], "rowb")
        pb = [Buf(pbt[i][:, :], f"pb{i}") for i in range(8)]

        class Alloc:
            def __init__(self, ap, cols):
                self.ap, self.cols, self.off = ap, cols, 0

            def buf(self, n, name=""):
                a = self.off
                self.off += n
                assert self.off <= self.cols, f"{name}: {self.off}>{self.cols}"
                return Buf(self.ap[:, a:a + n], name)

            def f32buf(self, n, name=""):
                a = self.off
                self.off += 2 * n
                assert self.off <= self.cols, f"{name}: {self.off}>{self.cols}"
                return Buf(self.ap[:, a:a + 2 * n].bitcast(F32), name)

            def reset(self):
                self.off = 0

        fa = Alloc(AFA[:, :], AF32_COLS)
        ba = Alloc(ABA[:, :], AB16_COLS)

        for dst, src in ((idf, ident_f32_d), (idb, ident_bf16_d),
                         (lm, lmat_d), (oc, onesc_d),
                         (orow, onesr_d), (md0, mdiag0_d), (md1, mdiag1_d),
                         (mg0, mgla0_d), (mg1, mgla1_d), (wsel, wsel_d)):
            prog.dma("sync", dst.ap, src, writes=(dst,))

        def split(src, hi, lo, src_ap=None, extra_reads=()):
            sap = src.ap if src_ap is None else src_ap
            prog.op("vector", lambda e, s=sap, h=hi.ap: e.tensor_copy(
                out=h, in_=s), reads=(src,) + tuple(extra_reads), writes=(hi,))
            return prog.op("vector", lambda e, s=sap, h=hi.ap, l=lo.ap:
                           e.tensor_tensor(out=l, in0=s, in1=h, op=ALU.subtract),
                           reads=(src, hi) + tuple(extra_reads), writes=(lo,))

        def ln_token(x, o, sqtmp, cols, eps, sc, x_ap=None, o_ap=None):
            xap = x.ap if x_ap is None else x_ap
            oap = o.ap if o_ap is None else o_ap
            prog.op("vector", lambda e, xx=xap: e.tensor_reduce(
                out=stat.ap[:, sc:sc + 1], in_=xx, axis=AX.X, op=ALU.add),
                reads=(x,), writes=(stat,))
            prog.op("scalar", lambda e, xx=xap, t=sqtmp.ap: e.activation(
                out=t, in_=xx, func=AFT.Square,
                accum_out=stat.ap[:, sc + 1:sc + 2]),
                reads=(x,), writes=(sqtmp, stat))
            prog.op("vector", lambda e, c=cols: e.tensor_single_scalar(
                out=stat.ap[:, sc + 2:sc + 3], in_=stat.ap[:, sc:sc + 1],
                scalar=1.0 / c, op=ALU.mult), reads=(stat,), writes=(stat,))
            prog.op("vector", lambda e: e.tensor_tensor(
                out=stat.ap[:, sc + 3:sc + 4], in0=stat.ap[:, sc + 2:sc + 3],
                in1=stat.ap[:, sc + 2:sc + 3], op=ALU.mult),
                reads=(stat,), writes=(stat,))
            prog.op("vector", lambda e, c=cols, ep=eps: e.tensor_scalar(
                out=stat.ap[:, sc + 4:sc + 5], in0=stat.ap[:, sc + 1:sc + 2],
                scalar1=1.0 / c, scalar2=ep, op0=ALU.mult, op1=ALU.add),
                reads=(stat,), writes=(stat,))
            prog.op("vector", lambda e: e.tensor_tensor(
                out=stat.ap[:, sc + 4:sc + 5], in0=stat.ap[:, sc + 4:sc + 5],
                in1=stat.ap[:, sc + 3:sc + 4], op=ALU.subtract),
                reads=(stat,), writes=(stat,))
            prog.op("scalar", lambda e: e.activation(
                out=stat.ap[:, sc + 5:sc + 6], in_=stat.ap[:, sc + 4:sc + 5],
                func=AFT.Sqrt), reads=(stat,), writes=(stat,))
            prog.op("vector", lambda e: e.reciprocal(
                out=stat.ap[:, sc + 6:sc + 7], in_=stat.ap[:, sc + 5:sc + 6]),
                reads=(stat,), writes=(stat,))
            prog.op("vector", lambda e, xx=xap, oo=oap: e.tensor_scalar(
                out=oo, in0=xx, scalar1=stat.ap[:, sc + 2:sc + 3],
                scalar2=stat.ap[:, sc + 6:sc + 7],
                op0=ALU.subtract, op1=ALU.mult),
                reads=(x, stat), writes=(o,))

        def transpose_round(src, src_cols, bank, ident=None, extra_reads=()):
            idn = idf if ident is None else ident
            for j, c0 in enumerate(src_cols):
                prog.op("tensor",
                        lambda e, o=bank.ap[:, j * 128:(j + 1) * 128],
                        i=src.ap[:, c0:c0 + 128], n=idn.ap: e.transpose(o, i, n),
                        reads=(src, idn) + tuple(extra_reads),
                        writes=(bank,))

        def mm(bank, lhs_buf, lhs_ap, rhs_buf, rhs_ap, start, stop,
               out_ap=None, extra_reads=()):
            o = bank.ap if out_ap is None else out_ap
            reads = [lhs_buf, rhs_buf] + list(extra_reads)
            prog.op("tensor", lambda e, oo=o, l=lhs_ap, r=rhs_ap, s=start,
                    z=stop: e.matmul(oo, l, r, start=s, stop=z),
                    reads=reads, writes=(bank,))

        def emit_rope(kbuf, obuf, tbuf, kap, oap, tap, nh):
            k3 = kap.rearrange("p (h s d) -> p h s d", h=nh, s=2)
            o3 = oap.rearrange("p (h s d) -> p h s d", h=nh, s=2)
            t3 = tap.rearrange("p (h s d) -> p h s d", h=nh, s=2)
            x1, x2 = k3[:, :, 0, :], k3[:, :, 1, :]
            o1, o2 = o3[:, :, 0, :], o3[:, :, 1, :]
            t1, t2 = t3[:, :, 0, :], t3[:, :, 1, :]
            ch = ropeb.ap[:, 0:64].unsqueeze(1).broadcast_to((128, nh, 64))
            cl = ropeb.ap[:, 64:128].unsqueeze(1).broadcast_to((128, nh, 64))
            sh = ropeb.ap[:, 128:192].unsqueeze(1).broadcast_to((128, nh, 64))
            sl = ropeb.ap[:, 192:256].unsqueeze(1).broadcast_to((128, nh, 64))
            def tt(out_ap, a_ap, b_ap, opx, reads, writes):
                prog.op("vector", lambda e, o=out_ap, a=a_ap, b=b_ap, op_=opx:
                        e.tensor_tensor(out=o, in0=a, in1=b, op=op_),
                        reads=reads, writes=writes)
            R = (kbuf, ropeb)
            tt(o1, x1, ch, ALU.mult, R, (obuf,))
            tt(t1, x2, sh, ALU.mult, R, (tbuf,))
            tt(o1, o1, t1, ALU.subtract, (obuf, tbuf), (obuf,))
            tt(t1, x1, cl, ALU.mult, R, (tbuf,))
            tt(o1, o1, t1, ALU.add, (obuf, tbuf), (obuf,))
            tt(t1, x2, sl, ALU.mult, R, (tbuf,))
            tt(o1, o1, t1, ALU.subtract, (obuf, tbuf), (obuf,))
            tt(o2, x2, ch, ALU.mult, R, (obuf,))
            tt(t2, x1, sh, ALU.mult, R, (tbuf,))
            tt(o2, o2, t2, ALU.add, (obuf, tbuf), (obuf,))
            tt(t2, x2, cl, ALU.mult, R, (tbuf,))
            tt(o2, o2, t2, ALU.add, (obuf, tbuf), (obuf,))
            tt(t2, x1, sl, ALU.mult, R, (tbuf,))
            tt(o2, o2, t2, ALU.add, (obuf, tbuf), (obuf,))

        ld0 = fa.buf(2048, "ld0")
        ld1 = fa.buf(2048, "ld1")
        sg0 = fa.buf(2048, "sg0")
        sg1 = fa.buf(2048, "sg1")
        ev0 = ba.buf(512, "ev0")
        ev1 = ba.buf(512, "ev1")
        wh0 = ba.buf(2048, "wh0")
        wl0 = ba.buf(2048, "wl0")

        for dc in range(16):
            buf = (ld0, ld1)[dc % 2]
            prog.dma("sync", buf.ap, wo[dc * 128:(dc + 1) * 128, :], writes=(buf,))
            split(buf, wh0, wl0)
            prog.dma("gpsimd", woh[dc * 128:(dc + 1) * 128, :], wh0.ap,
                     reads=(wh0,))
            prog.dma("gpsimd", wol[dc * 128:(dc + 1) * 128, :], wl0.ap,
                     reads=(wl0,))

        for wsrc, wdst in ((wg, wgT), (wu, wuT)):
            for hc in range(FH // 128):
                buf = (ld0, ld1)[hc % 2]
                sgn = (sg0, sg1)[hc % 2]
                prog.dma("sync", buf.ap, wsrc[hc * 128:(hc + 1) * 128, :],
                         writes=(buf,))
                prog.op("scalar", lambda e, i=buf.ap, o=sgn.ap: e.sign(out=o, in_=i),
                        reads=(buf,), writes=(sgn,))
                for rnd in range(4):
                    bank = pb[rnd % 2]
                    ev = (ev0, ev1)[rnd % 2]
                    transpose_round(sgn, [(rnd * 4 + j) * 128 for j in range(4)],
                                    bank)
                    prog.op("vector", lambda e, b=bank.ap, o=ev.ap:
                            e.tensor_copy(out=o, in_=b), reads=(bank,),
                            writes=(ev,))
                    prog.dma("gpsimd",
                             wdst[(hc * 16 + rnd * 4) * 128:
                                  (hc * 16 + rnd * 4 + 4) * 128, :]
                             .rearrange("(a p) h -> p a h", p=128),
                             ev.ap.rearrange("p (a h) -> p a h", a=4),
                             reads=(ev,))

        for dc2 in range(16):
            for hq in range(4):
                buf = (ld0, ld1)[hq % 2]
                sgn = (sg0, sg1)[hq % 2]
                prog.dma("sync", buf.ap,
                         wd[dc2 * 128:(dc2 + 1) * 128, hq * 2048:(hq + 1) * 2048],
                         writes=(buf,))
                prog.op("scalar", lambda e, i=buf.ap, o=sgn.ap: e.sign(out=o, in_=i),
                        reads=(buf,), writes=(sgn,))
                for rnd in range(4):
                    bank = pb[rnd % 2]
                    ev = (ev0, ev1)[rnd % 2]
                    transpose_round(sgn, [(rnd * 4 + j) * 128 for j in range(4)],
                                    bank)
                    prog.op("vector", lambda e, b=bank.ap, o=ev.ap:
                            e.tensor_copy(out=o, in_=b), reads=(bank,),
                            writes=(ev,))
                    for j in range(4):
                        hc = hq * 16 + rnd * 4 + j
                        prog.dma("gpsimd",
                                 wdT[(dc2 * 64 + hc) * 128:
                                     (dc2 * 64 + hc + 1) * 128, :],
                                 ev.ap[:, j * 128:(j + 1) * 128], reads=(ev,))
        prog.fence()
        fa.reset()
        ba.reset()

        xt = fa.buf(2048, "xt")
        xn = fa.buf(2048, "xn")
        sqt = fa.buf(2048, "sqt")
        wstg = fa.buf(512, "wstg")
        kp = fa.buf(512, "kp")
        skf = fa.buf(512, "skf")
        ksf = fa.buf(512, "ksf")
        spf = fa.buf(512, "spf")
        elf = fa.buf(512, "elf")
        enf = fa.buf(512, "enf")
        ktf = fa.buf(512, "ktf")
        khf = fa.buf(512, "khf")
        wk_hi = ba.buf(8192, "wk_hi")
        wk_lo = ba.buf(8192, "wk_lo")
        wv_hi = ba.buf(8192, "wv_hi")
        wv_lo = ba.buf(8192, "wv_lo")
        xnT_hi = ba.buf(2048, "xnT_hi")
        xnT_lo = ba.buf(2048, "xnT_lo")
        bsv = [ba.buf(512, f"bsv{i}") for i in range(2)]
        bsk = [ba.buf(512, f"bsk{i}") for i in range(2)]
        bkt = [ba.buf(512, f"bkt{i}") for i in range(2)]
        bkh = [ba.buf(512, f"bkh{i}") for i in range(2)]
        bsp = [ba.buf(512, f"bsp{i}") for i in range(2)]

        for wsrc, whi, wlo in ((wk, wk_hi, wk_lo), (wv, wv_hi, wv_lo)):
            for dc in range(16):
                prog.dma("sync", wstg.ap, wsrc[dc * 128:(dc + 1) * 128, :],
                         writes=(wstg,))
                prog.op("vector", lambda e, s=wstg.ap,
                        h=whi.ap[:, dc * 512:(dc + 1) * 512]:
                        e.tensor_copy(out=h, in_=s),
                        reads=(wstg,), writes=(whi,))
                prog.op("vector", lambda e, s=wstg.ap,
                        h=whi.ap[:, dc * 512:(dc + 1) * 512],
                        l=wlo.ap[:, dc * 512:(dc + 1) * 512]:
                        e.tensor_tensor(out=l, in0=s, in1=h, op=ALU.subtract),
                        reads=(wstg, whi), writes=(wlo,))

        for ib in range(NB):
            r0 = ib * 128
            prog.dma("sync", xt.ap, xb[r0:r0 + 128, :], writes=(xt,))
            for j, nm in enumerate(("cosk_hi", "cosk_lo", "sink_hi", "sink_lo")):
                prog.dma("sync", ropeb.ap[:, j * 64:(j + 1) * 64],
                         ropes[nm][r0:r0 + 128, :], writes=(ropeb,))
            ln_token(xt, xn, sqt, 2048, EPS1, 0)
            for rnd in range(4):
                bank = pb[rnd % 2]
                transpose_round(xn, [(rnd * 4 + j) * 128 for j in range(4)], bank)
                prog.op("vector", lambda e, b=bank.ap,
                        h=xnT_hi.ap[:, rnd * 512:(rnd + 1) * 512]:
                        e.tensor_copy(out=h, in_=b),
                        reads=(bank,), writes=(xnT_hi,))
                prog.op("vector", lambda e, b=bank.ap,
                        h=xnT_hi.ap[:, rnd * 512:(rnd + 1) * 512],
                        l=xnT_lo.ap[:, rnd * 512:(rnd + 1) * 512]:
                        e.tensor_tensor(out=l, in0=b, in1=h, op=ALU.subtract),
                        reads=(bank, xnT_hi), writes=(xnT_lo,))
            if "dbg_xn" in dbg:
                prog.dma("gpsimd", dbg_xn[r0:r0 + 128, :], xn.ap, reads=(xn,))
                prog.dma("gpsimd", dbg_xnT_hi[r0:r0 + 128, :], xnT_hi.ap,
                         reads=(xnT_hi,))
                prog.dma("gpsimd", dbg_xnT_lo[r0:r0 + 128, :], xnT_lo.ap,
                         reads=(xnT_lo,))
            for bank, whi, wlo in ((pb[2], wk_hi, wk_lo), (pb[3], wv_hi, wv_lo)):
                nmm = 0
                for xs, ws in ((xnT_hi, whi), (xnT_lo, whi), (xnT_hi, wlo)):
                    for dc in range(16):
                        nmm += 1
                        mm(bank, xs, xs.ap[:, dc * 128:(dc + 1) * 128],
                           ws, ws.ap[:, dc * 512:(dc + 1) * 512],
                           start=(nmm == 1), stop=(nmm == 48))
            prog.op("vector", lambda e, o=kp.ap: e.tensor_copy(out=o, in_=pb[2].ap),
                    reads=(pb[2],), writes=(kp,))
            if "dbg_xn" in dbg:
                prog.dma("gpsimd", dbg_k[r0:r0 + 128, :], kp.ap, reads=(kp,))
            split(pb[3], bsv[0], bsv[1])
            prog.dma("gpsimd", svs_hi[r0:r0 + 128, :], bsv[0].ap, reads=(bsv[0],))
            prog.dma("gpsimd", svs_lo[r0:r0 + 128, :], bsv[1].ap, reads=(bsv[1],))

            emit_rope(kp, skf, sqt, kp.ap, skf.ap, sqt.ap[:, 0:512], 4)

            transpose_round(skf, [j * 128 for j in range(4)], pb[4])
            split(pb[4], bsk[0], bsk[1])
            prog.dma("gpsimd",
                     skT_hi[:, r0:r0 + 128].rearrange("(h d) t -> d h t", d=128),
                     bsk[0].ap.rearrange("p (h t) -> p h t", h=4), reads=(bsk[0],))
            prog.dma("gpsimd",
                     skT_lo[:, r0:r0 + 128].rearrange("(h d) t -> d h t", d=128),
                     bsk[1].ap.rearrange("p (h t) -> p h t", h=4), reads=(bsk[1],))

            prog.op("vector", lambda e, kk=kp.ap: e.tensor_reduce(
                out=stat.ap[:, 8:12], in_=kk.rearrange("p (h d) -> p h d", h=4),
                axis=AX.X, op=ALU.max, negate=True),
                reads=(kp,), writes=(stat,))
            for hh in range(4):
                prog.op("scalar", lambda e, kk=kp.ap, oo=ksf.ap, h=hh:
                        e.activation(out=oo[:, h * 128:(h + 1) * 128],
                                     in_=kk[:, h * 128:(h + 1) * 128],
                                     func=AFT.Exp, bias=stat.ap[:, 8 + h:9 + h],
                                     scale=1.0,
                                     accum_out=stat.ap[:, 12 + h:13 + h]),
                        reads=(kp, stat), writes=(ksf, stat))
            prog.op("vector", lambda e: e.reciprocal(
                out=stat.ap[:, 16:20], in_=stat.ap[:, 12:16]),
                reads=(stat,), writes=(stat,))
            for hh in range(4):
                prog.op("vector", lambda e, oo=ksf.ap, h=hh:
                        e.tensor_single_scalar(out=oo[:, h * 128:(h + 1) * 128],
                                               in_=oo[:, h * 128:(h + 1) * 128],
                                               scalar=stat.ap[:, 16 + h:17 + h],
                                               op=ALU.mult),
                        reads=(ksf, stat), writes=(ksf,))

            prog.op("scalar", lambda e, kk=kp.ap, oo=enf.ap: e.activation(
                out=oo, in_=kk, func=AFT.Exp, scale=-1.0),
                reads=(kp,), writes=(enf,))
            prog.op("scalar", lambda e, oo=spf.ap: e.activation(
                out=oo, in_=enf.ap, func=AFT.Ln, bias=1.0),
                reads=(enf,), writes=(spf,))
            split(spf, bsp[0], bsp[1])
            mm(pb[5], lm, lm.ap, bsp[0], bsp[0].ap, start=True, stop=False)
            mm(pb[5], lm, lm.ap, bsp[1], bsp[1].ap, start=False, stop=True)

            prog.op("scalar", lambda e, oo=elf.ap: e.activation(
                out=oo, in_=pb[5].ap, func=AFT.Exp), reads=(pb[5],), writes=(elf,))
            prog.dma("gpsimd", expLam[r0:r0 + 128, :], elf.ap, reads=(elf,))
            prog.op("scalar", lambda e, oo=enf.ap: e.activation(
                out=oo, in_=pb[5].ap, func=AFT.Exp, scale=-1.0),
                reads=(pb[5],), writes=(enf,))

            prog.dma("gpsimd", rowb.ap[0:1, :], elf.ap[127:128, :],
                     reads=(elf,), writes=(rowb,))
            mm(pb[6], orow, orow.ap, rowb, rowb.ap[0:1, :], start=True, stop=True)
            for hh in range(4):
                mm(pb[7], rowb, rowb.ap[0:1, hh * 128:(hh + 1) * 128],
                   orow, orow.ap[0:1, 0:1], start=True, stop=True,
                   out_ap=pb[7].ap[:, 504 + hh:505 + hh])
            prog.op("vector", lambda e, c0=24: e.tensor_copy(
                out=stat.ap[:, c0:c0 + 4], in_=pb[7].ap[:, 504:508]),
                reads=(pb[7],), writes=(stat,))
            prog.dma("gpsimd", dCT[:, ib * 4:(ib + 1) * 4], stat.ap[:, 24:28],
                     reads=(stat,))

            prog.op("vector", lambda e, oo=ktf.ap: e.tensor_tensor(
                out=oo, in0=ksf.ap, in1=enf.ap, op=ALU.mult),
                reads=(ksf, enf), writes=(ktf,))
            prog.op("vector", lambda e, oo=khf.ap: e.tensor_tensor(
                out=oo, in0=ktf.ap, in1=pb[6].ap, op=ALU.mult),
                reads=(ktf, pb[6]), writes=(khf,))
            split(khf, bkh[0], bkh[1])
            prog.dma("gpsimd", khat_hi[r0:r0 + 128, :], bkh[0].ap, reads=(bkh[0],))
            prog.dma("gpsimd", khat_lo[r0:r0 + 128, :], bkh[1].ap, reads=(bkh[1],))
            transpose_round(ktf, [j * 128 for j in range(4)], pb[4])
            split(pb[4], bkt[0], bkt[1])
            prog.dma("gpsimd",
                     ktT_hi[:, r0:r0 + 128].rearrange("(h d) t -> d h t", d=128),
                     bkt[0].ap.rearrange("p (h t) -> p h t", h=4), reads=(bkt[0],))
            prog.dma("gpsimd",
                     ktT_lo[:, r0:r0 + 128].rearrange("(h d) t -> d h t", d=128),
                     bkt[1].ap.rearrange("p (h t) -> p h t", h=4), reads=(bkt[1],))

        prog.fence()
        fa.reset()
        ba.reset()

        xt = fa.buf(2048, "xt2")
        xn = fa.buf(2048, "xn2")
        sqt = fa.buf(2048, "sqt2")
        qp = fa.buf(1024, "qp")
        sqf = fa.buf(1024, "sqf")
        qtf = fa.buf(1024, "qtf")
        el0 = fa.buf(512, "el0")
        el1 = fa.buf(512, "el1")
        wq_hi = ba.buf(16 * 1024, "wq_hi")
        wq_lo = ba.buf(16 * 1024, "wq_lo")
        xnT_hi = ba.buf(2048, "xnT_hi2")
        xnT_lo = ba.buf(2048, "xnT_lo2")
        bq = [ba.buf(512, f"bq{i}") for i in range(2)]
        bq2 = [ba.buf(512, f"bq2{i}") for i in range(2)]

        for half in range(2):
            hof = half * 1024
            for dc in range(16):
                prog.dma("sync", sqt.ap[:, 0:1024],
                         wq[dc * 128:(dc + 1) * 128, hof:hof + 1024],
                         writes=(sqt,))
                prog.op("vector", lambda e, s=sqt.ap[:, 0:1024],
                        h=wq_hi.ap[:, dc * 1024:(dc + 1) * 1024]:
                        e.tensor_copy(out=h, in_=s),
                        reads=(sqt,), writes=(wq_hi,))
                prog.op("vector", lambda e, s=sqt.ap[:, 0:1024],
                        h=wq_hi.ap[:, dc * 1024:(dc + 1) * 1024],
                        l=wq_lo.ap[:, dc * 1024:(dc + 1) * 1024]:
                        e.tensor_tensor(out=l, in0=s, in1=h, op=ALU.subtract),
                        reads=(sqt, wq_hi), writes=(wq_lo,))

            for idx in range(NOWN):
                t0 = idx * 128
                prog.dma("sync", xt.ap, xown[t0:t0 + 128, :], writes=(xt,))
                for j, nm in enumerate(("ocosq_hi", "ocosq_lo", "osinq_hi",
                                        "osinq_lo")):
                    prog.dma("sync", ropeb.ap[:, j * 64:(j + 1) * 64],
                             oropes[nm][t0:t0 + 128, :], writes=(ropeb,))
                prog.dma("sync", el0.ap,
                         expLam[(2 * idx) * 128:(2 * idx + 1) * 128, :],
                         writes=(el0,))
                prog.dma("sync", el1.ap,
                         expLam[(2 * idx + 1) * 128:(2 * idx + 2) * 128, :],
                         writes=(el1,))
                ln_token(xt, xn, sqt, 2048, EPS1, 0)
                for rnd in range(4):
                    bank = pb[rnd % 2]
                    transpose_round(xn, [(rnd * 4 + j) * 128 for j in range(4)],
                                    bank)
                    prog.op("vector", lambda e, b=bank.ap,
                            h=xnT_hi.ap[:, rnd * 512:(rnd + 1) * 512]:
                            e.tensor_copy(out=h, in_=b),
                            reads=(bank,), writes=(xnT_hi,))
                    prog.op("vector", lambda e, b=bank.ap,
                            h=xnT_hi.ap[:, rnd * 512:(rnd + 1) * 512],
                            l=xnT_lo.ap[:, rnd * 512:(rnd + 1) * 512]:
                            e.tensor_tensor(out=l, in0=b, in1=h,
                                            op=ALU.subtract),
                            reads=(bank, xnT_hi), writes=(xnT_lo,))
                for jt in range(2):
                    bank = pb[2 + jt]
                    nmm = 0
                    for xs, ws in ((xnT_hi, wq_hi), (xnT_lo, wq_hi),
                                   (xnT_hi, wq_lo)):
                        for dc in range(16):
                            nmm += 1
                            mm(bank, xs, xs.ap[:, dc * 128:(dc + 1) * 128],
                               ws, ws.ap[:, dc * 1024 + jt * 512:
                                         dc * 1024 + jt * 512 + 512],
                               start=(nmm == 1), stop=(nmm == 48))
                def qc(e, o=qp.ap):
                    for jt in range(2):
                        e.tensor_copy(out=o[:, jt * 512:(jt + 1) * 512],
                                      in_=pb[2 + jt].ap)
                    return None
                prog.op("vector", qc, reads=(pb[2], pb[3]), writes=(qp,))

                emit_rope(qp, sqf, sqt, qp.ap, sqf.ap, sqt.ap[:, 0:1024], 8)

                prog.op("vector", lambda e, kk=qp.ap: e.tensor_reduce(
                    out=stat.ap[:, 8:16],
                    in_=kk.rearrange("p (h d) -> p h d", h=8),
                    axis=AX.X, op=ALU.max, negate=True),
                    reads=(qp,), writes=(stat,))
                for hh in range(8):
                    prog.op("scalar", lambda e, kk=qp.ap, oo=qtf.ap, h=hh:
                            e.activation(out=oo[:, h * 128:(h + 1) * 128],
                                         in_=kk[:, h * 128:(h + 1) * 128],
                                         func=AFT.Exp,
                                         bias=stat.ap[:, 8 + h:9 + h],
                                         scale=1.0,
                                         accum_out=stat.ap[:, 16 + h:17 + h]),
                            reads=(qp, stat), writes=(qtf, stat))
                prog.op("vector", lambda e: e.reciprocal(
                    out=stat.ap[:, 24:32], in_=stat.ap[:, 16:24]),
                    reads=(stat,), writes=(stat,))
                for hh in range(8):
                    prog.op("vector", lambda e, oo=qtf.ap, h=hh:
                            e.tensor_single_scalar(
                                out=oo[:, h * 128:(h + 1) * 128],
                                in_=oo[:, h * 128:(h + 1) * 128],
                                scalar=stat.ap[:, 24 + h:25 + h], op=ALU.mult),
                            reads=(qtf, stat), writes=(qtf,))
                eof = half * 256
                prog.op("vector", lambda e, eo=eof: e.tensor_single_scalar(
                    out=sqt.ap[:, 0:256], in_=el0.ap[:, eo:eo + 256],
                    scalar=wsel.ap[:, 0:1], op=ALU.mult),
                    reads=(el0, wsel), writes=(sqt,))
                prog.op("vector", lambda e, eo=eof: e.scalar_tensor_tensor(
                    out=sqt.ap[:, 0:256], in0=el1.ap[:, eo:eo + 256],
                    scalar=wsel.ap[:, 1:2], in1=sqt.ap[:, 0:256],
                    op0=ALU.mult, op1=ALU.add),
                    reads=(el1, wsel, sqt), writes=(sqt,))
                prog.op("vector", lambda e, oo=qtf.ap: e.tensor_tensor(
                    out=oo.rearrange("p (g r d) -> p g r d", g=2, r=4),
                    in0=oo.rearrange("p (g r d) -> p g r d", g=2, r=4),
                    in1=sqt.ap[:, 0:256].rearrange("p (g d) -> p g d", g=2)
                    .unsqueeze(2).broadcast_to((128, 2, 4, 128)), op=ALU.mult),
                    reads=(qtf, sqt), writes=(qtf,))

                for src_, bufs, dsts in ((sqf, bq, (sqT_hi, sqT_lo)),
                                         (qtf, bq2, (qtT_hi, qtT_lo))):
                    for rnd in range(2):
                        bank = pb[6 + rnd % 2]
                        transpose_round(src_,
                                        [(rnd * 4 + j) * 128 for j in range(4)],
                                        bank)
                        split(bank, bufs[0], bufs[1])
                        for z, dst in enumerate(dsts):
                            prog.dma("gpsimd",
                                     dst[:, t0 * 16 + half * 1024 + rnd * 512:
                                         t0 * 16 + half * 1024 + (rnd + 1) * 512],
                                     bufs[z].ap, reads=(bufs[z],))
        prog.fence()
        fa.reset()
        ba.reset()

        ogla = fa.buf(16 * 512, "ogla")
        Sst = fa.buf(128, "Sst")
        Ssc = fa.buf(128, "Ssc")
        Pf = [fa.buf(512, f"Pf{i}") for i in range(2)]
        osf = fa.buf(512, "osf")
        amf = fa.buf(512, "amf")
        skh = ba.buf(4096, "skh")
        skl = ba.buf(4096, "skl")
        svh = ba.buf(4096, "svh")
        svl = ba.buf(4096, "svl")
        khh = ba.buf(4096, "khh")
        khl = ba.buf(4096, "khl")
        kth = ba.buf(4096, "kth")
        ktl = ba.buf(4096, "ktl")
        qth = [ba.buf(512, f"qth{i}") for i in range(2)]
        qtl = [ba.buf(512, f"qtl{i}") for i in range(2)]
        sqh = [ba.buf(512, f"sqh{i}") for i in range(2)]
        sql = [ba.buf(512, f"sql{i}") for i in range(2)]
        Shi = ba.buf(128, "Shi")
        Slo = ba.buf(128, "Slo")
        Phi = [ba.buf(512, f"Phi{i}") for i in range(2)]
        Plo = [ba.buf(512, f"Plo{i}") for i in range(2)]
        Ahi = ba.buf(512, "Ahi")
        Alo = ba.buf(512, "Alo")
        obh = [ba.buf(512, f"obh{i}") for i in range(2)]
        obl = [ba.buf(512, f"obl{i}") for i in range(2)]

        prog.dma("sync", dct.ap, dCT, reads=(), writes=(dct,))

        for g in range(KV):
            prog.dma("sync", skh.ap, skT_hi[g * 128:(g + 1) * 128, :], writes=(skh,))
            prog.dma("sync", skl.ap, skT_lo[g * 128:(g + 1) * 128, :], writes=(skl,))
            prog.dma("sync", kth.ap, ktT_hi[g * 128:(g + 1) * 128, :], writes=(kth,))
            prog.dma("sync", ktl.ap, ktT_lo[g * 128:(g + 1) * 128, :], writes=(ktl,))
            prog.dma("sync", svh.ap.rearrange("p (c d) -> p c d", c=32),
                     svs_hi.rearrange("(c p) e -> p c e", p=128)
                     [:, :, g * 128:(g + 1) * 128], writes=(svh,))
            prog.dma("sync", svl.ap.rearrange("p (c d) -> p c d", c=32),
                     svs_lo.rearrange("(c p) e -> p c e", p=128)
                     [:, :, g * 128:(g + 1) * 128], writes=(svl,))
            prog.dma("sync", khh.ap.rearrange("p (c d) -> p c d", c=32),
                     khat_hi.rearrange("(c p) e -> p c e", p=128)
                     [:, :, g * 128:(g + 1) * 128], writes=(khh,))
            prog.dma("sync", khl.ap.rearrange("p (c d) -> p c d", c=32),
                     khat_lo.rearrange("(c p) e -> p c e", p=128)
                     [:, :, g * 128:(g + 1) * 128], writes=(khl,))

            for c in range(NB):
                idx = c // 2
                delta = c % 2
                mg = (mg0, mg1)[delta]
                for pi, (l_, r_) in enumerate(((khh, svh), (khl, svh), (khh, svl))):
                    mm(pb[5], l_, l_.ap[:, c * 128:(c + 1) * 128],
                       r_, r_.ap[:, c * 128:(c + 1) * 128],
                       start=(pi == 0), stop=(pi == 2),
                       out_ap=pb[5].ap[:, 0:128])
                if delta == 0:
                    prog.dma("sync", qth[idx % 2].ap,
                             qtT_hi[:, idx * 2048 + g * 512:
                                    idx * 2048 + (g + 1) * 512],
                             writes=(qth[idx % 2],))
                    prog.dma("sync", qtl[idx % 2].ap,
                             qtT_lo[:, idx * 2048 + g * 512:
                                    idx * 2048 + (g + 1) * 512],
                             writes=(qtl[idx % 2],))
                qh, ql = qth[idx % 2], qtl[idx % 2]
                oglab = pb[3 + idx % 2]
                first_mm_done = (delta == 1)
                if c > 0:
                    def ssc(e, dcol=delta):
                        return e.tensor_single_scalar(
                            out=Ssc.ap, in_=Sst.ap,
                            scalar=wsel.ap[:, dcol:dcol + 1], op=ALU.mult)
                    prog.op("vector", ssc, reads=(Sst, wsel), writes=(Ssc,))
                    split(Ssc, Shi, Slo)
                for pi, (l_, r_) in enumerate(((kth, qh), (ktl, qh), (kth, ql))):
                    mm(pb[6], l_, l_.ap[:, c * 128:(c + 1) * 128], r_, r_.ap,
                       start=(pi == 0), stop=(pi == 2))
                if c > 0:
                    for pi, (l_, r_) in enumerate(((Shi, qh), (Slo, qh),
                                                   (Shi, ql))):
                        mm(oglab, l_, l_.ap, r_, r_.ap,
                           start=(delta == 0 and pi == 0), stop=False)
                        first_mm_done = True
                prog.op("vector", lambda e, m=mg.ap: e.tensor_tensor(
                    out=amf.ap, in0=pb[6].ap, in1=m, op=ALU.mult),
                    reads=(pb[6], mg), writes=(amf,))
                split(amf, Ahi, Alo)
                for pi, (l_, r_) in enumerate((
                        (svh, Ahi), (svl, Ahi), (svh, Alo))):
                    mm(oglab, l_, l_.ap[:, c * 128:(c + 1) * 128], r_, r_.ap,
                       start=(c == 0 and pi == 0),
                       stop=(delta == 1 and pi == 2))
                if delta == 1:
                    prog.op("vector", lambda e, o=ogla.ap[:, idx * 512:(idx + 1) * 512],
                            b=oglab.ap: e.tensor_copy(out=o, in_=b),
                            reads=(oglab,), writes=(ogla,))
                if c == 0:
                    prog.op("vector", lambda e: e.tensor_copy(
                        out=Sst.ap, in_=pb[5].ap[:, 0:128]),
                        reads=(pb[5],), writes=(Sst,))
                else:
                    def upd(e, col=c * 4 + g):
                        return e.scalar_tensor_tensor(
                            out=Sst.ap, in0=Sst.ap,
                            scalar=dct.ap[:, col:col + 1],
                            in1=pb[5].ap[:, 0:128], op0=ALU.mult, op1=ALU.add)
                    prog.op("vector", upd, reads=(Sst, dct, pb[5]), writes=(Sst,))

            if "dbg_ogla" in dbg:
                prog.dma("gpsimd", dbg_ogla[g * 128:(g + 1) * 128, :], ogla.ap,
                         reads=(ogla,))
            for idx in range(NOWN):
                qpar = idx % 2
                prog.dma("sync", sqh[qpar].ap,
                         sqT_hi[:, idx * 2048 + g * 512:idx * 2048 + (g + 1) * 512],
                         writes=(sqh[qpar],))
                prog.dma("sync", sql[qpar].ap,
                         sqT_lo[:, idx * 2048 + g * 512:idx * 2048 + (g + 1) * 512],
                         writes=(sql[qpar],))
                J = 2 * idx + 2
                for j in range(J):
                    jp = j % 2
                    scb = pb[jp]
                    for pi, (l_, r_) in enumerate((
                            (skh, sqh[qpar]), (skl, sqh[qpar]), (skh, sql[qpar]))):
                        mm(scb, l_, l_.ap[:, j * 128:(j + 1) * 128], r_, r_.ap,
                           start=(pi == 0), stop=(pi == 2))
                    prog.op("scalar", lambda e, o=Pf[jp].ap, b=scb.ap:
                            e.activation(out=o, in_=b, func=AFT.Exp),
                            reads=(scb,), writes=(Pf[jp],))
                    if j >= J - 2:
                        mmask = (md0, md1)[j - (J - 2)]
                        prog.op("vector", lambda e, o=Pf[jp].ap, m=mmask.ap:
                                e.tensor_tensor(out=o, in0=o, in1=m, op=ALU.mult),
                                reads=(Pf[jp], mmask), writes=(Pf[jp],))
                    split(Pf[jp], Phi[jp], Plo[jp])
                    for z, Ps in enumerate((Phi[jp], Plo[jp])):
                        mm(pb[7], oc, oc.ap, Ps, Ps.ap,
                           start=(j == 0 and z == 0), stop=(j == J - 1 and z == 1),
                           out_ap=pb[7].ap[0:1, :])
                    for pi, (l_, r_) in enumerate((
                            (svh, Phi[jp]), (svl, Phi[jp]), (svh, Plo[jp]))):
                        mm(pb[2], l_, l_.ap[:, j * 128:(j + 1) * 128], r_, r_.ap,
                           start=(j == 0 and pi == 0),
                           stop=(j == J - 1 and pi == 2))
                if "dbg_l" in dbg:
                    prog.op("vector", lambda e: e.tensor_copy(
                        out=amf.ap[0:1, :], in_=pb[7].ap[0:1, :]),
                        reads=(pb[7],), writes=(amf,))
                    prog.dma("gpsimd", dbg_l[g * NOWN + idx:g * NOWN + idx + 1, :],
                             amf.ap[0:1, :], reads=(amf,))
                prog.op("vector", lambda e: e.tensor_copy(
                    out=rowb.ap[0:1, :], in_=pb[7].ap[0:1, :]),
                    reads=(pb[7],), writes=(rowb,))
                prog.op("vector", lambda e: e.reciprocal(
                    out=rowb.ap[0:1, :], in_=rowb.ap[0:1, :]),
                    reads=(rowb,), writes=(rowb,))
                prog.op("vector", lambda e: e.tensor_single_scalar(
                    out=rowb.ap[0:1, :], in_=rowb.ap[0:1, :], scalar=0.5,
                    op=ALU.mult), reads=(rowb,), writes=(rowb,))
                mm(pb[6], orow, orow.ap, rowb, rowb.ap[0:1, :], start=True,
                   stop=True)
                prog.op("vector", lambda e: e.tensor_copy(
                    out=amf.ap, in_=pb[6].ap), reads=(pb[6],), writes=(amf,))
                prog.op("vector", lambda e: e.tensor_tensor(
                    out=osf.ap, in0=pb[2].ap, in1=amf.ap, op=ALU.mult),
                    reads=(pb[2], amf), writes=(osf,))
                prog.op("vector", lambda e,
                        og=ogla.ap[:, idx * 512:(idx + 1) * 512]:
                        e.scalar_tensor_tensor(out=osf.ap, in0=og, scalar=0.5,
                                               in1=osf.ap, op0=ALU.mult,
                                               op1=ALU.add),
                        reads=(ogla, osf), writes=(osf,))
                split(osf, obh[qpar], obl[qpar])
                for z, (bufv, dst) in enumerate(((obh[qpar], oT_hi),
                                                 (obl[qpar], oT_lo))):
                    prog.dma("gpsimd",
                             dst.rearrange("p (h t) -> p h t", h=16)
                             [:, 4 * g:4 * g + 4, idx * 128:(idx + 1) * 128],
                             bufv.ap.rearrange("p (h t) -> p h t", h=4),
                             reads=(bufv,))
        prog.fence()
        fa.reset()
        ba.reset()

        featb = fa.buf(512, "featb")
        xt4 = fa.buf(2048, "xt4")
        xn4 = fa.buf(2048, "xn4")
        gb4 = fa.buf(2048, "gb4")
        wfl = fa.buf(2048, "wfl")
        rem4 = fa.buf(2048, "rem4")
        o_h = ba.buf(16 * 512, "o_h")
        o_l = ba.buf(16 * 512, "o_l")
        wo_h = [ba.buf(2048, f"wo_h{i}") for i in range(2)]
        wo_l = [ba.buf(2048, f"wo_l{i}") for i in range(2)]
        xqb = [ba.buf(512, f"xqb{i}") for i in range(2)]
        atok = ba.f32buf(4 * 2048, "atok")

        for g4 in range(4):
            prog.dma("sync", o_h.ap.rearrange("p (h t) -> p h t", h=16),
                     oT_hi.rearrange("p (h t) -> p h t", h=16)
                     [:, :, g4 * 512:(g4 + 1) * 512], writes=(o_h,))
            prog.dma("sync", o_l.ap.rearrange("p (h t) -> p h t", h=16),
                     oT_lo.rearrange("p (h t) -> p h t", h=16)
                     [:, :, g4 * 512:(g4 + 1) * 512], writes=(o_l,))
            for jc in range(16):
                jp = jc % 2
                prog.dma("sync", wo_h[jp].ap.rearrange("p (a j) -> p a j", a=16),
                         woh.rearrange("(a p) j -> p a j", p=128)
                         [:, :, jc * 128:(jc + 1) * 128], writes=(wo_h[jp],))
                prog.dma("sync", wo_l[jp].ap.rearrange("p (a j) -> p a j", a=16),
                         wol.rearrange("(a p) j -> p a j", p=128)
                         [:, :, jc * 128:(jc + 1) * 128], writes=(wo_l[jp],))
                bank = pb[jp]
                nmm = 0
                for os_, ws_ in ((o_h, wo_h[jp]), (o_l, wo_h[jp]),
                                 (o_h, wo_l[jp])):
                    for dc in range(16):
                        nmm += 1
                        mm(bank, ws_, ws_.ap[:, dc * 128:(dc + 1) * 128],
                           os_, os_.ap[:, dc * 512:(dc + 1) * 512],
                           start=(nmm == 1), stop=(nmm == 48))
                prog.op("vector", lambda e, b=bank.ap: e.tensor_copy(
                    out=featb.ap, in_=b), reads=(bank,), writes=(featb,))
                tbank = pb[2 + jp]
                transpose_round(featb, [j * 128 for j in range(4)], tbank)
                prog.op("vector", lambda e, b=tbank.ap, col=jc * 128:
                        e.tensor_copy(
                            out=atok.ap.rearrange("p (b j) -> p b j", b=4)
                            [:, :, col:col + 128],
                            in_=b.rearrange("p (b j) -> p b j", b=4)),
                        reads=(tbank,), writes=(atok,))

            for tsub in range(4):
                idx = g4 * 4 + tsub
                t0 = idx * 128
                prog.dma("sync", xt4.ap, xown[t0:t0 + 128, :], writes=(xt4,))
                ln_token(atok, gb4, rem4, 2048, EPS1, 0,
                         x_ap=atok.ap[:, tsub * 2048:(tsub + 1) * 2048])
                prog.op("vector", lambda e: e.tensor_tensor(
                    out=xn4.ap, in0=xt4.ap, in1=gb4.ap, op=ALU.add),
                    reads=(xt4, gb4), writes=(xn4,))
                prog.dma("gpsimd", x1s[t0:t0 + 128, :], xn4.ap, reads=(xn4,))
                ln_token(xn4, gb4, rem4, 2048, EPS2, 8)
                prog.op("vector", lambda e: e.tensor_reduce(
                    out=stat.ap[:, 30:31], in_=gb4.ap, axis=AX.X, op=ALU.max,
                    apply_absolute_value=True), reads=(gb4,), writes=(stat,))
                prog.op("vector", lambda e: e.reciprocal(
                    out=stat.ap[:, 31:32], in_=stat.ap[:, 30:31]),
                    reads=(stat,), writes=(stat,))
                prog.op("vector", lambda e: e.tensor_single_scalar(
                    out=stat.ap[:, 31:32], in_=stat.ap[:, 31:32], scalar=127.0,
                    op=ALU.mult), reads=(stat,), writes=(stat,))
                prog.op("vector", lambda e: e.tensor_single_scalar(
                    out=stat.ap[:, 32:33], in_=stat.ap[:, 30:31],
                    scalar=1.0 / 127.0, op=ALU.mult), reads=(stat,), writes=(stat,))
                prog.op("vector", lambda e: e.tensor_scalar(
                    out=wfl.ap, in0=gb4.ap, scalar1=stat.ap[:, 31:32],
                    scalar2=128.5, op0=ALU.mult, op1=ALU.add),
                    reads=(gb4, stat), writes=(wfl,))
                prog.dma("gpsimd", sinvs[t0:t0 + 128, :], stat.ap[:, 32:33],
                         reads=(stat,))
                if "dbg_gb" in dbg:
                    prog.dma("gpsimd", dbg_gb[t0:t0 + 128, :], gb4.ap, reads=(gb4,))
                    prog.dma("gpsimd", dbg_w[t0:t0 + 128, :], wfl.ap, reads=(wfl,))
                prog.op("vector", lambda e: e.tensor_copy(
                    out=rem4.ap, in_=wfl.ap), reads=(wfl,), writes=(rem4,))
                for bbit in (128.0, 64.0, 32.0, 16.0, 8.0, 4.0, 2.0, 1.0):
                    prog.op("vector", lambda e, bb=bbit: e.tensor_scalar(
                        out=gb4.ap, in0=rem4.ap, scalar1=bb, scalar2=bb,
                        op0=ALU.is_ge, op1=ALU.mult),
                        reads=(rem4,), writes=(gb4,))
                    prog.op("vector", lambda e: e.tensor_tensor(
                        out=rem4.ap, in0=rem4.ap, in1=gb4.ap, op=ALU.subtract),
                        reads=(rem4, gb4), writes=(rem4,))
                prog.op("vector", lambda e: e.scalar_tensor_tensor(
                    out=wfl.ap, in0=wfl.ap, scalar=-128.0, in1=rem4.ap,
                    op0=ALU.add, op1=ALU.subtract),
                    reads=(wfl, rem4), writes=(wfl,))
                for rnd in range(4):
                    tbank = pb[4 + rnd % 2]
                    transpose_round(wfl, [(rnd * 4 + j) * 128 for j in range(4)],
                                    tbank)
                    prog.op("vector", lambda e, b=tbank.ap, q=xqb[rnd % 2].ap:
                            e.tensor_copy(out=q, in_=b),
                            reads=(tbank,), writes=(xqb[rnd % 2],))
                    prog.dma("gpsimd",
                             xqT[(rnd * 4) * 128:(rnd * 4 + 4) * 128, t0:t0 + 128]
                             .rearrange("(a p) t -> p a t", p=128),
                             xqb[rnd % 2].ap.rearrange("p (a t) -> p a t", a=4),
                             reads=(xqb[rnd % 2],))
        prog.fence()
        fa.reset()
        ba.reset()

        guv_s = fa.buf(512, "guv")
        guv = [guv_s, guv_s]
        slv = fa.buf(512, "slv")
        pflo = fa.buf(512, "pflo")
        sinv_t = fa.buf(512, "sinv_t")
        dacc = fa.buf(16 * 512, "dacc")
        ftok = fa.buf(2048, "ftok")
        x1t = fa.buf(2048, "x1t")
        lnb = fa.buf(2048, "lnb")
        sqt5 = lnb
        xq_s = ba.buf(16 * 512, "xq_s")
        p_h = [ba.buf(512, f"p_h{i}") for i in range(32)]
        p_l = [ba.buf(512, f"p_l{i}") for i in range(32)]
        wgs = [ba.buf(2048, f"wgs{i}") for i in range(2)]
        wus = [ba.buf(2048, f"wus{i}") for i in range(2)]

        for g5 in range(4):
            t0 = g5 * 512
            prog.dma("sync", xq_s.ap.rearrange("p (a t) -> p a t", a=16),
                     xqT.rearrange("(a p) t -> p a t", p=128)[:, :, t0:t0 + 512],
                     writes=(xq_s,))
            prog.dma("sync", stat.ap[:, 34:38],
                     sinvs[t0:t0 + 512, :].rearrange("(a p) o -> p (a o)", p=128),
                     reads=(stat,), writes=(stat,), slow=True)
            for ts in range(4):
                mm(pb[6], stat, stat.ap[:, 34 + ts:35 + ts], idf, idf.ap,
                   start=True, stop=True,
                   out_ap=pb[6].ap[0:1, ts * 128:(ts + 1) * 128])
            prog.op("vector", lambda e: e.tensor_copy(out=rowb.ap[0:1, :],
                                                      in_=pb[6].ap[0:1, :]),
                    reads=(pb[6],), writes=(rowb,))
            mm(pb[6], orow, orow.ap, rowb, rowb.ap[0:1, :], start=True, stop=True)
            prog.op("vector", lambda e: e.tensor_copy(out=sinv_t.ap,
                                                      in_=pb[6].ap),
                    reads=(pb[6],), writes=(sinv_t,))

            for hh in range(2):
                for hl in range(32):
                    hc = hh * 32 + hl
                    hp = hl % 2
                    prog.dma("sync", wgs[hp].ap.rearrange("p (a h) -> p a h", a=16),
                             wgT[hc * 2048:(hc + 1) * 2048, :]
                             .rearrange("(a p) h -> p a h", p=128),
                             writes=(wgs[hp],))
                    prog.dma("sync", wus[hp].ap.rearrange("p (a h) -> p a h", a=16),
                             wuT[hc * 2048:(hc + 1) * 2048, :]
                             .rearrange("(a p) h -> p a h", p=128),
                             writes=(wus[hp],))
                    for z, (wsb, bank) in enumerate(((wgs[hp], pb[0 + hp]),
                                                     (wus[hp], pb[2 + hp]))):
                        for dc in range(16):
                            mm(bank, wsb, wsb.ap[:, dc * 128:(dc + 1) * 128],
                               xq_s, xq_s.ap[:, dc * 512:(dc + 1) * 512],
                               start=(dc == 0), stop=(dc == 15))
                    gbank, ubank = pb[0 + hp], pb[2 + hp]
                    prog.op("vector", lambda e, gu=guv[hp].ap, b=gbank.ap:
                            e.tensor_tensor(out=gu, in0=b, in1=sinv_t.ap,
                                            op=ALU.mult),
                            reads=(gbank, sinv_t), writes=(guv[hp],))
                    prog.op("scalar", lambda e, gu=guv[hp].ap: e.activation(
                        out=slv.ap, in_=gu, func=AFT.Silu),
                        reads=(guv[hp],), writes=(slv,))
                    prog.op("vector", lambda e, b=ubank.ap: e.tensor_tensor(
                        out=pflo.ap, in0=slv.ap, in1=b, op=ALU.mult),
                        reads=(slv, ubank), writes=(pflo,))
                    split(pflo, p_h[hl], p_l[hl])

                for ds in range(16):
                    dp = ds % 2
                    prog.dma("sync", wgs[dp].ap.rearrange("p (a h) -> p a h", a=16),
                             wdT[(ds * 64 + hh * 32) * 128:
                                 (ds * 64 + hh * 32 + 16) * 128, :]
                             .rearrange("(a p) h -> p a h", p=128),
                             writes=(wgs[dp],))
                    prog.dma("sync", wus[dp].ap.rearrange("p (a h) -> p a h", a=16),
                             wdT[(ds * 64 + hh * 32 + 16) * 128:
                                 (ds * 64 + hh * 32 + 32) * 128, :]
                             .rearrange("(a p) h -> p a h", p=128),
                             writes=(wus[dp],))
                    bank = pb[4 + dp]
                    nmm = 0
                    for hl in range(32):
                        wsb = wgs[dp] if hl < 16 else wus[dp]
                        wcol = (hl % 16) * 128
                        for Ps in (p_h[hl], p_l[hl]):
                            nmm += 1
                            mm(bank, wsb, wsb.ap[:, wcol:wcol + 128], Ps, Ps.ap,
                               start=(nmm == 1), stop=(nmm == 64))
                    if hh == 0:
                        prog.op("vector", lambda e,
                                o=dacc.ap[:, ds * 512:(ds + 1) * 512], b=bank.ap:
                                e.tensor_copy(out=o, in_=b),
                                reads=(bank,), writes=(dacc,))
                    else:
                        prog.op("vector", lambda e,
                                o=dacc.ap[:, ds * 512:(ds + 1) * 512], b=bank.ap:
                                e.tensor_tensor(out=o, in0=o, in1=b, op=ALU.add),
                                reads=(bank, dacc), writes=(dacc,))

            for tsub in range(4):
                idx = g5 * 4 + tsub
                t0b = idx * 128
                for rnd in range(4):
                    tbank = pb[6 + rnd % 2]
                    transpose_round(dacc,
                                    [(rnd * 4 + j) * 512 + tsub * 128
                                     for j in range(4)], tbank)
                    prog.op("vector", lambda e, b=tbank.ap, r=rnd:
                            e.tensor_copy(out=ftok.ap[:, r * 512:(r + 1) * 512],
                                          in_=b),
                            reads=(tbank,), writes=(ftok,))
                prog.dma("sync", x1t.ap, x1s[t0b:t0b + 128, :], writes=(x1t,))
                ln_token(ftok, lnb, sqt5, 2048, EPS1, 0)
                prog.op("vector", lambda e: e.tensor_tensor(
                    out=x1t.ap, in0=x1t.ap, in1=lnb.ap, op=ALU.add),
                    reads=(x1t, lnb), writes=(x1t,))
                prog.dma("gpsimd", out[t0b:t0b + 128, :], x1t.ap, reads=(x1t,))

        last_stores = [o.oid for o in prog.ops if o.is_dma and o.engine == "gpsimd"]
        prog.op("gpsimd", lambda e: None, extra_deps=(last_stores[-1],))

        block = es.enter_context(nc.Block())
        prog.emit(nc, es, block)

    return nc


_CONSTS = None


def _shard_inputs(inputs):
    global _CONSTS
    if _CONSTS is None:
        _CONSTS = _host_constants()
    consts = _CONSTS
    x = np.ascontiguousarray(np.asarray(inputs["L0_attn_hidden_states"],
                                        dtype=np.float32))
    w = {k: np.ascontiguousarray(np.asarray(inputs[k2], dtype=np.float32))
         for k, k2 in (("wq", "L0_attn_Wq"), ("wk", "L0_attn_Wk"),
                       ("wv", "L0_attn_Wv"), ("wo", "L0_attn_Wo"),
                       ("wg", "L0_ffn_W_g"), ("wu", "L0_ffn_W_u"),
                       ("wd", "L0_ffn_W_d"))}
    shared = {k: consts[k] for k in ("ident_f32", "ident_bf16", "lmat", "ones_col_bf16",
                                     "ones_row_f32", "cosk_hi", "cosk_lo",
                                     "sink_hi", "sink_lo")}
    par_in = [_parity_inputs(0, consts), _parity_inputs(1, consts)]
    maps = []
    for c in range(8):
        b, par = c // 2, c % 2
        pi = par_in[par]
        m = dict(xb=np.ascontiguousarray(x[b]),
                 xown=np.ascontiguousarray(x[b][pi["_idxs"]]))
        m.update(w)
        m.update(shared)
        m.update({k: v for k, v in pi.items() if not k.startswith("_")})
        maps.append(m)
    return maps


def kernel(**inputs):
    maps = _shard_inputs(inputs)
    nc = build_program()
    res = run_bass_kernel_spmd(nc, maps, core_ids=list(range(8)))
    outp = np.zeros((B, N, D), dtype=np.float32)
    par_in = [_parity_inputs(0, _CONSTS), _parity_inputs(1, _CONSTS)]
    for c in range(8):
        b, par = c // 2, c % 2
        outp[b].reshape(-1, D)[par_in[par]["_idxs"]] = res.results[c]["out"]
    return outp


# version 03
# speedup vs baseline: 1.0779x; 1.0779x over previous
import contextlib
import numpy as np
import ml_dtypes
import concourse.bass as bass
import concourse.mybir as mybir
from concourse.bass_utils import run_bass_kernel_spmd

F32 = mybir.dt.float32
BF16 = mybir.dt.bfloat16
AFT = mybir.ActivationFunctionType
ALU = mybir.AluOpType
AX = mybir.AxisListType

B, N, D = 4, 4096, 2048
H, KV, HD = 16, 4, 128
NB = N // 128
NOWN = NB // 2
FH = 8192
ROPE_THETA = 10000.0
GATE_NORM = 16.0
SCALE = HD ** (-0.5)
EPS1 = 1e-5
EPS2 = 1e-8

AF32_COLS = 18432
AB16_COLS = 50176

ENGINES = ("tensor", "vector", "scalar", "sync", "gpsimd")


class Op:
    __slots__ = ("oid", "engine", "fn", "deps", "inc", "seq", "is_dma")

    def __init__(self, oid, engine, fn, deps, is_dma):
        self.oid = oid
        self.engine = engine
        self.fn = fn
        self.deps = deps
        self.inc = False
        self.seq = 0
        self.is_dma = is_dma


class Buf:
    __slots__ = ("ap", "writer", "readers", "name")

    def __init__(self, ap, name=""):
        self.ap = ap
        self.writer = None
        self.readers = []
        self.name = name

    def __getitem__(self, sl):
        return self.ap[sl]


class Prog:
    def __init__(self):
        self.ops = []
        self.fence_deps = ()
        self.last_op = {}

    def fence(self):
        last = {}
        for o in self.ops:
            last[o.engine] = o.oid
        self.fence_deps = tuple(last.values())

    def op(self, engine, fn, reads=(), writes=(), extra_deps=(), is_dma=False):
        deps = set(extra_deps)
        deps.update(self.fence_deps)
        for b in reads:
            if b.writer is not None:
                deps.add(b.writer)
        for b in writes:
            if b.writer is not None:
                deps.add(b.writer)
            deps.update(b.readers)
        oid = len(self.ops)
        o = Op(oid, engine, fn, deps, is_dma)
        self.ops.append(o)
        self.last_op[engine] = oid
        for b in reads:
            b.readers.append(oid)
        for b in writes:
            b.writer = oid
            b.readers = []
        return oid

    def dma(self, engine, out_ap, in_ap, reads=(), writes=(), extra_deps=(),
            slow=False):
        return self.op(engine,
                       lambda e, o=out_ap, i=in_ap, sl=slow: e.dma_start(
                           out=o, in_=i, allow_slow_non_contiguous=sl),
                       reads=reads, writes=writes, extra_deps=extra_deps,
                       is_dma=True)

    def emit(self, nc, es, block):
        ops = self.ops
        by_engine = {e: [o for o in ops if o.engine == e] for e in ENGINES}
        for e in ENGINES:
            for i, o in enumerate(by_engine[e]):
                o.seq = i
        WIN = 1 << 30
        for o in ops:
            kept = set()
            for d in o.deps:
                if ops[d].engine != o.engine:
                    kept.add(d)
                elif ops[d].is_dma or o.is_dma:
                    kept.add(d)
                elif o.seq - ops[d].seq <= WIN:
                    kept.add(d)
            o.deps = kept
            for d in o.deps:
                ops[d].inc = True
            if o.is_dma:
                o.inc = True
        sems = {e: es.enter_context(nc.semaphore(f"sem_{e}")) for e in ENGINES}
        cnt = {e: 0 for e in ENGINES}
        post = {}
        for e in ENGINES:
            c = 0
            for o in by_engine[e]:
                if o.inc:
                    c += 16 if o.is_dma else 1
                post[o.oid] = c
        for e in ENGINES:
            seen = {e2: 0 for e2 in ENGINES}
            for o in by_engine[e]:
                ws = {}
                for d in o.deps:
                    pe = ops[d].engine
                    v = post[d]
                    if v > seen[pe]:
                        ws[pe] = max(ws.get(pe, 0), v)
                for pe, v in ws.items():
                    seen[pe] = v
                o.deps = sorted(ws.items())
        for ename in ENGINES:
            eops = by_engine[ename]
            if not eops:
                continue
            def mkbody(eops_):
                def body(eng):
                    for o in eops_:
                        for pe, v in o.deps:
                            eng.wait_ge(sems[pe], v)
                        inst = o.fn(eng)
                        if o.inc:
                            if inst is None:
                                inst = eng.nop()
                            inst.then_inc(sems[ename], 16 if o.is_dma else 1)
                return body
            getattr(block, ename)(mkbody(list(eops)))


def _host_constants():
    c = {}
    c["ident_f32"] = np.eye(128, dtype=np.float32)
    c["ident_bf16"] = np.eye(128, dtype=ml_dtypes.bfloat16)
    c["lmat"] = np.triu(np.full((128, 128), -1.0 / GATE_NORM)).astype(
        ml_dtypes.bfloat16)
    c["ones_col_bf16"] = np.ones((128, 1), dtype=ml_dtypes.bfloat16)
    c["ones_row_f32"] = np.ones((1, 128), dtype=np.float32)
    tril = (np.arange(128)[None, :] >= np.arange(128)[:, None]).astype(np.float32)
    c["_tril4"] = np.ascontiguousarray(np.tile(tril, (1, 4)))
    pos = np.arange(N, dtype=np.float64)[:, None]
    inv = 1.0 / (ROPE_THETA ** (np.arange(0, HD, 2, dtype=np.float64) / HD))
    fr = pos * inv[None, :]
    for nm, tab in (("cosk", np.cos(fr)), ("sink", np.sin(fr)),
                    ("cosq", np.cos(fr) * SCALE), ("sinq", np.sin(fr) * SCALE)):
        hi = tab.astype(np.float32)
        lo = (tab - hi.astype(np.float64)).astype(np.float32)
        c[nm + "_hi"] = np.ascontiguousarray(hi)
        c[nm + "_lo"] = np.ascontiguousarray(lo)
    return c


def _parity_inputs(par, consts):
    d = {}
    ones = np.ones((128, 512), dtype=np.float32)
    zeros = np.zeros((128, 512), dtype=np.float32)
    tril4 = consts["_tril4"]
    d["mdiag0"] = tril4 if par == 0 else ones
    d["mdiag1"] = zeros if par == 0 else tril4
    d["mgla0"] = tril4 if par == 0 else zeros
    d["mgla1"] = zeros if par == 0 else tril4
    w = np.zeros((128, 2), dtype=np.float32)
    w[:, par] = 1.0
    d["wsel"] = w
    idxs = ((2 * np.arange(NOWN)[:, None] + par) * 128
            + np.arange(128)[None, :]).ravel()
    for nm in ("cosq_hi", "cosq_lo", "sinq_hi", "sinq_lo"):
        d["o" + nm] = np.ascontiguousarray(consts[nm][idxs])
    d["_idxs"] = idxs
    return d


def build_program(dbg=()):
    nc = bass.Bass("TRN2", target_bir_lowering=False)

    def di(name, shape, dt):
        return nc.dram_tensor(name, list(shape), dt, kind="ExternalInput").ap()

    def drm(name, shape, dt):
        kind = "ExternalOutput" if name in dbg else "Internal"
        return nc.dram_tensor(name, list(shape), dt, kind=kind).ap()

    xb = di("xb", (N, D), F32)
    xown = di("xown", (NOWN * 128, D), F32)
    wq = di("wq", (D, D), F32)
    wk = di("wk", (D, KV * HD), F32)
    wv = di("wv", (D, KV * HD), F32)
    wo = di("wo", (D, D), F32)
    wg = di("wg", (FH, D), F32)
    wu = di("wu", (FH, D), F32)
    wd = di("wd", (D, FH), F32)
    ident_f32_d = di("ident_f32", (128, 128), F32)
    ident_bf16_d = di("ident_bf16", (128, 128), BF16)
    lmat_d = di("lmat", (128, 128), BF16)
    onesc_d = di("ones_col_bf16", (128, 1), BF16)
    onesr_d = di("ones_row_f32", (1, 128), F32)
    mdiag0_d = di("mdiag0", (128, 512), F32)
    mdiag1_d = di("mdiag1", (128, 512), F32)
    mgla0_d = di("mgla0", (128, 512), F32)
    mgla1_d = di("mgla1", (128, 512), F32)
    wsel_d = di("wsel", (128, 2), F32)
    ropes = {nm: di(nm, (N, 64), F32)
             for nm in ("cosk_hi", "cosk_lo", "sink_hi", "sink_lo")}
    oropes = {nm: di(nm, (NOWN * 128, 64), F32)
              for nm in ("ocosq_hi", "ocosq_lo", "osinq_hi", "osinq_lo")}

    out = nc.dram_tensor("out", [NOWN * 128, D], F32, kind="ExternalOutput").ap()

    skT_hi = drm("skT_hi", (KV * HD, N), BF16)
    skT_lo = drm("skT_lo", (KV * HD, N), BF16)
    ktT_hi = drm("ktT_hi", (KV * HD, N), BF16)
    ktT_lo = drm("ktT_lo", (KV * HD, N), BF16)
    khat_hi = drm("khat_hi", (N, KV * HD), BF16)
    khat_lo = drm("khat_lo", (N, KV * HD), BF16)
    svs_hi = drm("svs_hi", (N, KV * HD), BF16)
    svs_lo = drm("svs_lo", (N, KV * HD), BF16)
    expLam = drm("expLam", (N, KV * HD), F32)
    dCT = drm("dCT", (128, NB * KV), F32)
    sqT_hi = drm("sqT_hi", (128, NOWN * H * 128), BF16)
    sqT_lo = drm("sqT_lo", (128, NOWN * H * 128), BF16)
    qtT_hi = drm("qtT_hi", (128, NOWN * H * 128), BF16)
    qtT_lo = drm("qtT_lo", (128, NOWN * H * 128), BF16)
    oT_hi = drm("oT_hi", (128, H * NOWN * 128), BF16)
    oT_lo = drm("oT_lo", (128, H * NOWN * 128), BF16)
    woh = drm("woh", (D, D), BF16)
    wol = drm("wol", (D, D), BF16)
    wgT = drm("wgT", (FH // 128 * 16 * 128, 128), BF16)
    wuT = drm("wuT", (FH // 128 * 16 * 128, 128), BF16)
    wdT = drm("wdT", (16 * 64 * 128, 128), BF16)
    x1s = drm("x1s", (NOWN * 128, D), F32)
    dbg_xn = drm("dbg_xn", (N, D), F32)
    dbg_xnT_hi = drm("dbg_xnT_hi", (N, 2048), BF16)
    dbg_xnT_lo = drm("dbg_xnT_lo", (N, 2048), BF16)
    dbg_k = drm("dbg_k", (N, 512), F32)
    dbg_ogla = drm("dbg_ogla", (KV * 128, 16 * 512), F32)
    dbg_l = drm("dbg_l", (KV * NOWN, 512), F32)
    dbg_gb = drm("dbg_gb", (NOWN * 128, D), F32)
    dbg_w = drm("dbg_w", (NOWN * 128, D), F32)
    xqT = drm("xqT", (16 * 128, NOWN * 128), BF16)
    sinvs = drm("sinvs", (NOWN * 128, 1), F32)

    prog = Prog()
    es = contextlib.ExitStack()

    with es:
        AFA = es.enter_context(nc.sbuf_tensor("af32", [128, AF32_COLS], F32))
        ABA = es.enter_context(nc.sbuf_tensor("ab16", [128, AB16_COLS], BF16))
        statT = es.enter_context(nc.sbuf_tensor("stat", [128, 128], F32))
        dct_s = es.enter_context(nc.sbuf_tensor("dct_s", [128, 128], F32))
        idf_s = es.enter_context(nc.sbuf_tensor("idf_s", [128, 128], F32))
        idb_s = es.enter_context(nc.sbuf_tensor("idb_s", [128, 128], BF16))
        lm_s = es.enter_context(nc.sbuf_tensor("lm_s", [128, 128], BF16))
        oc_s = es.enter_context(nc.sbuf_tensor("oc_s", [128, 1], BF16))
        or_s = es.enter_context(nc.sbuf_tensor("or_s", [1, 128], F32))
        md0_s = es.enter_context(nc.sbuf_tensor("md0_s", [128, 512], F32))
        md1_s = es.enter_context(nc.sbuf_tensor("md1_s", [128, 512], F32))
        mg0_s = es.enter_context(nc.sbuf_tensor("mg0_s", [128, 512], F32))
        mg1_s = es.enter_context(nc.sbuf_tensor("mg1_s", [128, 512], F32))
        wsel_s = es.enter_context(nc.sbuf_tensor("wsel_s", [128, 2], F32))
        rope_s = es.enter_context(nc.sbuf_tensor("rope_s", [128, 512], F32))
        row_s = es.enter_context(nc.sbuf_tensor("row_s", [1, 1024], F32))
        pbt = [es.enter_context(nc.psum_tensor(f"pb{i}", [128, 512], F32))
               for i in range(8)]

        stat = Buf(statT[:, 0:64], "stat")
        statp = [stat, Buf(statT[:, 64:128], "statB")]
        dct = Buf(dct_s[:, :], "dct")
        idf = Buf(idf_s[:, :], "idf")
        idb = Buf(idb_s[:, :], "idb")
        lm = Buf(lm_s[:, :], "lm")
        oc = Buf(oc_s[:, :], "oc")
        orow = Buf(or_s[:, :], "orow")
        md0 = Buf(md0_s[:, :], "md0")
        md1 = Buf(md1_s[:, :], "md1")
        mg0 = Buf(mg0_s[:, :], "mg0")
        mg1 = Buf(mg1_s[:, :], "mg1")
        wsel = Buf(wsel_s[:, :], "wsel")
        ropeb = Buf(rope_s[:, 0:256], "ropeb")
        ropep = [ropeb, Buf(rope_s[:, 256:512], "ropebB")]
        rowb = Buf(row_s[0:1, 0:512], "rowb")
        rowp = [rowb, Buf(row_s[0:1, 512:1024], "rowbB")]
        pb = [Buf(pbt[i][:, :], f"pb{i}") for i in range(8)]
        x1s_b = Buf(None, "x1s_dram")

        class Alloc:
            def __init__(self, ap, cols):
                self.ap, self.cols, self.off = ap, cols, 0

            def buf(self, n, name=""):
                a = self.off
                self.off += n
                assert self.off <= self.cols, f"{name}: {self.off}>{self.cols}"
                return Buf(self.ap[:, a:a + n], name)

            def f32buf(self, n, name=""):
                a = self.off
                self.off += 2 * n
                assert self.off <= self.cols, f"{name}: {self.off}>{self.cols}"
                return Buf(self.ap[:, a:a + 2 * n].bitcast(F32), name)

            def reset(self):
                self.off = 0

        fa = Alloc(AFA[:, :], AF32_COLS)
        ba = Alloc(ABA[:, :], AB16_COLS)

        for dst, src in ((idf, ident_f32_d), (idb, ident_bf16_d),
                         (lm, lmat_d), (oc, onesc_d),
                         (orow, onesr_d), (md0, mdiag0_d), (md1, mdiag1_d),
                         (mg0, mgla0_d), (mg1, mgla1_d), (wsel, wsel_d)):
            prog.dma("sync", dst.ap, src, writes=(dst,))

        def split(src, hi, lo, src_ap=None, extra_reads=()):
            sap = src.ap if src_ap is None else src_ap
            prog.op("vector", lambda e, s=sap, h=hi.ap: e.tensor_copy(
                out=h, in_=s), reads=(src,) + tuple(extra_reads), writes=(hi,))
            return prog.op("vector", lambda e, s=sap, h=hi.ap, l=lo.ap:
                           e.tensor_tensor(out=l, in0=s, in1=h, op=ALU.subtract),
                           reads=(src, hi) + tuple(extra_reads), writes=(lo,))

        def ln_token(x, o, sqtmp, cols, eps, sc, x_ap=None, o_ap=None, st=None):
            if st is None:
                st = stat
            xap = x.ap if x_ap is None else x_ap
            oap = o.ap if o_ap is None else o_ap
            prog.op("vector", lambda e, xx=xap: e.tensor_reduce(
                out=st.ap[:, sc:sc + 1], in_=xx, axis=AX.X, op=ALU.add),
                reads=(x,), writes=(st,))
            prog.op("scalar", lambda e, xx=xap, t=sqtmp.ap: e.activation(
                out=t, in_=xx, func=AFT.Square,
                accum_out=st.ap[:, sc + 1:sc + 2]),
                reads=(x,), writes=(sqtmp, st))
            prog.op("vector", lambda e, c=cols: e.tensor_single_scalar(
                out=st.ap[:, sc + 2:sc + 3], in_=st.ap[:, sc:sc + 1],
                scalar=1.0 / c, op=ALU.mult), reads=(st,), writes=(st,))
            prog.op("vector", lambda e: e.tensor_tensor(
                out=st.ap[:, sc + 3:sc + 4], in0=st.ap[:, sc + 2:sc + 3],
                in1=st.ap[:, sc + 2:sc + 3], op=ALU.mult),
                reads=(st,), writes=(st,))
            prog.op("vector", lambda e, c=cols, ep=eps: e.tensor_scalar(
                out=st.ap[:, sc + 4:sc + 5], in0=st.ap[:, sc + 1:sc + 2],
                scalar1=1.0 / c, scalar2=ep, op0=ALU.mult, op1=ALU.add),
                reads=(st,), writes=(st,))
            prog.op("vector", lambda e: e.tensor_tensor(
                out=st.ap[:, sc + 4:sc + 5], in0=st.ap[:, sc + 4:sc + 5],
                in1=st.ap[:, sc + 3:sc + 4], op=ALU.subtract),
                reads=(st,), writes=(st,))
            prog.op("scalar", lambda e: e.activation(
                out=st.ap[:, sc + 5:sc + 6], in_=st.ap[:, sc + 4:sc + 5],
                func=AFT.Sqrt), reads=(st,), writes=(st,))
            prog.op("vector", lambda e: e.reciprocal(
                out=st.ap[:, sc + 6:sc + 7], in_=st.ap[:, sc + 5:sc + 6]),
                reads=(st,), writes=(st,))
            prog.op("vector", lambda e, xx=xap, oo=oap: e.tensor_scalar(
                out=oo, in0=xx, scalar1=st.ap[:, sc + 2:sc + 3],
                scalar2=st.ap[:, sc + 6:sc + 7],
                op0=ALU.subtract, op1=ALU.mult),
                reads=(x, st), writes=(o,))

        def transpose_round(src, src_cols, bank, ident=None, extra_reads=()):
            idn = idf if ident is None else ident
            for j, c0 in enumerate(src_cols):
                prog.op("tensor",
                        lambda e, o=bank.ap[:, j * 128:(j + 1) * 128],
                        i=src.ap[:, c0:c0 + 128], n=idn.ap: e.transpose(o, i, n),
                        reads=(src, idn) + tuple(extra_reads),
                        writes=(bank,))

        def mm(bank, lhs_buf, lhs_ap, rhs_buf, rhs_ap, start, stop,
               out_ap=None, extra_reads=()):
            o = bank.ap if out_ap is None else out_ap
            reads = [lhs_buf, rhs_buf] + list(extra_reads)
            prog.op("tensor", lambda e, oo=o, l=lhs_ap, r=rhs_ap, s=start,
                    z=stop: e.matmul(oo, l, r, start=s, stop=z),
                    reads=reads, writes=(bank,))

        def emit_rope(kbuf, obuf, tbuf, kap, oap, tap, nh, rb=None):
            if rb is None:
                rb = ropeb
            k3 = kap.rearrange("p (h s d) -> p h s d", h=nh, s=2)
            o3 = oap.rearrange("p (h s d) -> p h s d", h=nh, s=2)
            t3 = tap.rearrange("p (h s d) -> p h s d", h=nh, s=2)
            x1, x2 = k3[:, :, 0, :], k3[:, :, 1, :]
            o1, o2 = o3[:, :, 0, :], o3[:, :, 1, :]
            t1, t2 = t3[:, :, 0, :], t3[:, :, 1, :]
            ch = rb.ap[:, 0:64].unsqueeze(1).broadcast_to((128, nh, 64))
            cl = rb.ap[:, 64:128].unsqueeze(1).broadcast_to((128, nh, 64))
            sh = rb.ap[:, 128:192].unsqueeze(1).broadcast_to((128, nh, 64))
            sl = rb.ap[:, 192:256].unsqueeze(1).broadcast_to((128, nh, 64))
            def tt(out_ap, a_ap, b_ap, opx, reads, writes):
                prog.op("vector", lambda e, o=out_ap, a=a_ap, b=b_ap, op_=opx:
                        e.tensor_tensor(out=o, in0=a, in1=b, op=op_),
                        reads=reads, writes=writes)
            R = (kbuf, rb)
            tt(o1, x1, ch, ALU.mult, R, (obuf,))
            tt(t1, x2, sh, ALU.mult, R, (tbuf,))
            tt(o1, o1, t1, ALU.subtract, (obuf, tbuf), (obuf,))
            tt(t1, x1, cl, ALU.mult, R, (tbuf,))
            tt(o1, o1, t1, ALU.add, (obuf, tbuf), (obuf,))
            tt(t1, x2, sl, ALU.mult, R, (tbuf,))
            tt(o1, o1, t1, ALU.subtract, (obuf, tbuf), (obuf,))
            tt(o2, x2, ch, ALU.mult, R, (obuf,))
            tt(t2, x1, sh, ALU.mult, R, (tbuf,))
            tt(o2, o2, t2, ALU.add, (obuf, tbuf), (obuf,))
            tt(t2, x2, cl, ALU.mult, R, (tbuf,))
            tt(o2, o2, t2, ALU.add, (obuf, tbuf), (obuf,))
            tt(t2, x1, sl, ALU.mult, R, (tbuf,))
            tt(o2, o2, t2, ALU.add, (obuf, tbuf), (obuf,))

        ldb = [fa.buf(2048, f"ld{i}") for i in range(4)]
        sgb = [fa.buf(2048, f"sg{i}") for i in range(4)]
        evb = [ba.buf(512, f"ev{i}") for i in range(4)]
        wh0 = ba.buf(2048, "wh0")
        wl0 = ba.buf(2048, "wl0")

        for dc in range(16):
            buf = ldb[dc % 4]
            prog.dma("sync", buf.ap, wo[dc * 128:(dc + 1) * 128, :], writes=(buf,))
            split(buf, wh0, wl0)
            prog.dma("gpsimd", woh[dc * 128:(dc + 1) * 128, :], wh0.ap,
                     reads=(wh0,))
            prog.dma("gpsimd", wol[dc * 128:(dc + 1) * 128, :], wl0.ap,
                     reads=(wl0,))

        kk = 0
        for wsrc, wdst, is_wd in ((wg, wgT, False), (wu, wuT, False),
                                  (wd, wdT, True)):
            nch = 64 if not is_wd else 64
            for hc in range(nch):
                buf = ldb[kk % 4]
                sgn = sgb[kk % 4]
                kk += 1
                if is_wd:
                    dc2, hq = hc // 4, hc % 4
                    srcap = wd[dc2 * 128:(dc2 + 1) * 128,
                               hq * 2048:(hq + 1) * 2048]
                else:
                    srcap = wsrc[hc * 128:(hc + 1) * 128, :]
                prog.dma("sync", buf.ap, srcap, writes=(buf,))
                prog.op("scalar", lambda e, i=buf.ap, o=sgn.ap: e.sign(
                    out=o, in_=i), reads=(buf,), writes=(sgn,))
                for rnd in range(4):
                    bank = pb[rnd]
                    ev = evb[rnd]
                    transpose_round(sgn, [(rnd * 4 + j) * 128 for j in range(4)],
                                    bank)
                    prog.op("vector", lambda e, b=bank.ap, o=ev.ap:
                            e.tensor_copy(out=o, in_=b), reads=(bank,),
                            writes=(ev,))
                    if not is_wd:
                        prog.dma("gpsimd",
                                 wdst[(hc * 16 + rnd * 4) * 128:
                                      (hc * 16 + rnd * 4 + 4) * 128, :]
                                 .rearrange("(a p) h -> p a h", p=128),
                                 ev.ap.rearrange("p (a h) -> p a h", a=4),
                                 reads=(ev,))
                    else:
                        for j in range(4):
                            hcc = hq * 16 + rnd * 4 + j
                            prog.dma("gpsimd",
                                     wdT[(dc2 * 64 + hcc) * 128:
                                         (dc2 * 64 + hcc + 1) * 128, :],
                                     ev.ap[:, j * 128:(j + 1) * 128],
                                     reads=(ev,))
        prog.fence()
        fa.reset()
        ba.reset()

        xtp = [fa.buf(2048, f"xt{i}") for i in range(2)]
        xnp = [fa.buf(2048, f"xn{i}") for i in range(2)]
        lndump = fa.buf(2048, "lndump")
        rtmp = fa.buf(512, "rtmp")
        wstg = fa.buf(512, "wstg")
        kpp = [fa.buf(512, f"kp{i}") for i in range(2)]
        skfp = [fa.buf(512, f"skf{i}") for i in range(2)]
        ksfp = [fa.buf(512, f"ksf{i}") for i in range(2)]
        spf = fa.buf(512, "spf")
        elfp = [fa.buf(512, f"elf{i}") for i in range(2)]
        enf = fa.buf(512, "enf")
        ktfp = [fa.buf(512, f"ktf{i}") for i in range(2)]
        khf = fa.buf(512, "khf")
        wk_hi = ba.buf(8192, "wk_hi")
        wk_lo = ba.buf(8192, "wk_lo")
        wv_hi = ba.buf(8192, "wv_hi")
        wv_lo = ba.buf(8192, "wv_lo")
        xnThp = [ba.buf(2048, f"xnTh{i}") for i in range(2)]
        xnTlp = [ba.buf(2048, f"xnTl{i}") for i in range(2)]
        bsv = [ba.buf(512, f"bsv{i}") for i in range(4)]
        bsk = [ba.buf(512, f"bsk{i}") for i in range(4)]
        bkt = [ba.buf(512, f"bkt{i}") for i in range(4)]
        bkh = [ba.buf(512, f"bkh{i}") for i in range(4)]
        bsp = [ba.buf(512, f"bsp{i}") for i in range(2)]

        for wsrc, whi, wlo in ((wk, wk_hi, wk_lo), (wv, wv_hi, wv_lo)):
            for dc in range(16):
                prog.dma("sync", wstg.ap, wsrc[dc * 128:(dc + 1) * 128, :],
                         writes=(wstg,))
                prog.op("vector", lambda e, s=wstg.ap,
                        h=whi.ap[:, dc * 512:(dc + 1) * 512]:
                        e.tensor_copy(out=h, in_=s),
                        reads=(wstg,), writes=(whi,))
                prog.op("vector", lambda e, s=wstg.ap,
                        h=whi.ap[:, dc * 512:(dc + 1) * 512],
                        l=wlo.ap[:, dc * 512:(dc + 1) * 512]:
                        e.tensor_tensor(out=l, in0=s, in1=h, op=ALU.subtract),
                        reads=(wstg, whi), writes=(wlo,))

        for ib in range(NB):
            par = ib % 2
            r0 = ib * 128
            xt, xn = xtp[par], xnp[par]
            kp, skf, ksf = kpp[par], skfp[par], ksfp[par]
            elf, ktf = elfp[par], ktfp[par]
            xnT_hi, xnT_lo = xnThp[par], xnTlp[par]
            st = statp[par]
            rb = ropep[par]
            rw = rowp[par]
            prog.dma("sync", xt.ap, xb[r0:r0 + 128, :], writes=(xt,))
            for j, nm in enumerate(("cosk_hi", "cosk_lo", "sink_hi", "sink_lo")):
                prog.dma("sync", rb.ap[:, j * 64:(j + 1) * 64],
                         ropes[nm][r0:r0 + 128, :], writes=(rb,))
            ln_token(xt, xn, lndump, 2048, EPS1, 0, st=st)
            for rnd in range(4):
                bank = pb[rnd % 2 + 2 * par] if False else pb[rnd % 2]
                transpose_round(xn, [(rnd * 4 + j) * 128 for j in range(4)], bank)
                prog.op("vector", lambda e, b=bank.ap,
                        h=xnT_hi.ap[:, rnd * 512:(rnd + 1) * 512]:
                        e.tensor_copy(out=h, in_=b),
                        reads=(bank,), writes=(xnT_hi,))
                prog.op("vector", lambda e, b=bank.ap,
                        h=xnT_hi.ap[:, rnd * 512:(rnd + 1) * 512],
                        l=xnT_lo.ap[:, rnd * 512:(rnd + 1) * 512]:
                        e.tensor_tensor(out=l, in0=b, in1=h, op=ALU.subtract),
                        reads=(bank, xnT_hi), writes=(xnT_lo,))
            if "dbg_xn" in dbg:
                prog.dma("gpsimd", dbg_xn[r0:r0 + 128, :], xn.ap, reads=(xn,))
                prog.dma("gpsimd", dbg_xnT_hi[r0:r0 + 128, :], xnT_hi.ap,
                         reads=(xnT_hi,))
                prog.dma("gpsimd", dbg_xnT_lo[r0:r0 + 128, :], xnT_lo.ap,
                         reads=(xnT_lo,))
            for bank, whi, wlo in ((pb[2 + par], wk_hi, wk_lo),
                                   (pb[4 + par], wv_hi, wv_lo)):
                nmm = 0
                for xs, ws in ((xnT_hi, whi), (xnT_lo, whi), (xnT_hi, wlo)):
                    for dc in range(16):
                        nmm += 1
                        mm(bank, xs, xs.ap[:, dc * 128:(dc + 1) * 128],
                           ws, ws.ap[:, dc * 512:(dc + 1) * 512],
                           start=(nmm == 1), stop=(nmm == 48))
            kbank, vbank = pb[2 + par], pb[4 + par]
            prog.op("vector", lambda e, o=kp.ap, b=kbank.ap:
                    e.tensor_copy(out=o, in_=b), reads=(kbank,), writes=(kp,))
            if "dbg_xn" in dbg:
                prog.dma("gpsimd", dbg_k[r0:r0 + 128, :], kp.ap, reads=(kp,))
            split(vbank, bsv[2 * par], bsv[2 * par + 1])
            prog.dma("gpsimd", svs_hi[r0:r0 + 128, :], bsv[2 * par].ap,
                     reads=(bsv[2 * par],))
            prog.dma("gpsimd", svs_lo[r0:r0 + 128, :], bsv[2 * par + 1].ap,
                     reads=(bsv[2 * par + 1],))

            emit_rope(kp, skf, rtmp, kp.ap, skf.ap, rtmp.ap, 4, rb=rb)

            tb2 = pb[6 + par]
            transpose_round(skf, [j * 128 for j in range(4)], tb2)
            split(tb2, bsk[2 * par], bsk[2 * par + 1])
            prog.dma("gpsimd",
                     skT_hi[:, r0:r0 + 128].rearrange("(h d) t -> d h t", d=128),
                     bsk[2 * par].ap.rearrange("p (h t) -> p h t", h=4),
                     reads=(bsk[2 * par],))
            prog.dma("gpsimd",
                     skT_lo[:, r0:r0 + 128].rearrange("(h d) t -> d h t", d=128),
                     bsk[2 * par + 1].ap.rearrange("p (h t) -> p h t", h=4),
                     reads=(bsk[2 * par + 1],))

            prog.op("vector", lambda e, kk=kp.ap, s=st: e.tensor_reduce(
                out=s.ap[:, 8:12], in_=kk.rearrange("p (h d) -> p h d", h=4),
                axis=AX.X, op=ALU.max, negate=True),
                reads=(kp,), writes=(st,))
            for hh in range(4):
                prog.op("scalar", lambda e, kk=kp.ap, oo=ksf.ap, h=hh, s=st:
                        e.activation(out=oo[:, h * 128:(h + 1) * 128],
                                     in_=kk[:, h * 128:(h + 1) * 128],
                                     func=AFT.Exp, bias=s.ap[:, 8 + h:9 + h],
                                     scale=1.0,
                                     accum_out=s.ap[:, 12 + h:13 + h]),
                        reads=(kp, st), writes=(ksf, st))
            prog.op("vector", lambda e, s=st: e.reciprocal(
                out=s.ap[:, 16:20], in_=s.ap[:, 12:16]),
                reads=(st,), writes=(st,))
            for hh in range(4):
                prog.op("vector", lambda e, oo=ksf.ap, h=hh, s=st:
                        e.tensor_single_scalar(out=oo[:, h * 128:(h + 1) * 128],
                                               in_=oo[:, h * 128:(h + 1) * 128],
                                               scalar=s.ap[:, 16 + h:17 + h],
                                               op=ALU.mult),
                        reads=(ksf, st), writes=(ksf,))

            prog.op("scalar", lambda e, kk=kp.ap, oo=enf.ap: e.activation(
                out=oo, in_=kk, func=AFT.Exp, scale=-1.0),
                reads=(kp,), writes=(enf,))
            prog.op("scalar", lambda e, oo=spf.ap: e.activation(
                out=oo, in_=enf.ap, func=AFT.Ln, bias=1.0),
                reads=(enf,), writes=(spf,))
            split(spf, bsp[0], bsp[1])
            lamb = pb[6 + par]
            mm(lamb, lm, lm.ap, bsp[0], bsp[0].ap, start=True, stop=False)
            mm(lamb, lm, lm.ap, bsp[1], bsp[1].ap, start=False, stop=True)

            prog.op("scalar", lambda e, oo=elf.ap, b=lamb.ap: e.activation(
                out=oo, in_=b, func=AFT.Exp), reads=(lamb,), writes=(elf,))
            prog.dma("gpsimd", expLam[r0:r0 + 128, :], elf.ap, reads=(elf,))
            prog.op("scalar", lambda e, oo=enf.ap, b=lamb.ap: e.activation(
                out=oo, in_=b, func=AFT.Exp, scale=-1.0),
                reads=(lamb,), writes=(enf,))

            prog.dma("gpsimd", rw.ap, elf.ap[127:128, :],
                     reads=(elf,), writes=(rw,))
            bcb = pb[2 + par]
            mm(bcb, orow, orow.ap, rw, rw.ap, start=True, stop=True)
            for hh in range(4):
                mm(pb[4 + par], rw, rw.ap[0:1, hh * 128:(hh + 1) * 128],
                   orow, orow.ap[0:1, 0:1], start=(hh == 0), stop=(hh == 3),
                   out_ap=pb[4 + par].ap[:, 504 + hh:505 + hh])
            prog.op("vector", lambda e, s=st, b=pb[4 + par].ap: e.tensor_copy(
                out=s.ap[:, 24:28], in_=b[:, 504:508]),
                reads=(pb[4 + par],), writes=(st,))
            prog.dma("gpsimd", dCT[:, ib * 4:(ib + 1) * 4], st.ap[:, 24:28],
                     reads=(st,))

            prog.op("vector", lambda e, oo=ktf.ap, kk=ksf.ap: e.tensor_tensor(
                out=oo, in0=kk, in1=enf.ap, op=ALU.mult),
                reads=(ksf, enf), writes=(ktf,))
            prog.op("vector", lambda e, oo=khf.ap, kk=ktf.ap, b=bcb.ap:
                    e.tensor_tensor(out=oo, in0=kk, in1=b, op=ALU.mult),
                    reads=(ktf, bcb), writes=(khf,))
            split(khf, bkh[2 * par], bkh[2 * par + 1])
            prog.dma("gpsimd", khat_hi[r0:r0 + 128, :], bkh[2 * par].ap,
                     reads=(bkh[2 * par],))
            prog.dma("gpsimd", khat_lo[r0:r0 + 128, :], bkh[2 * par + 1].ap,
                     reads=(bkh[2 * par + 1],))
            transpose_round(ktf, [j * 128 for j in range(4)], tb2)
            split(tb2, bkt[2 * par], bkt[2 * par + 1])
            prog.dma("gpsimd",
                     ktT_hi[:, r0:r0 + 128].rearrange("(h d) t -> d h t", d=128),
                     bkt[2 * par].ap.rearrange("p (h t) -> p h t", h=4),
                     reads=(bkt[2 * par],))
            prog.dma("gpsimd",
                     ktT_lo[:, r0:r0 + 128].rearrange("(h d) t -> d h t", d=128),
                     bkt[2 * par + 1].ap.rearrange("p (h t) -> p h t", h=4),
                     reads=(bkt[2 * par + 1],))

        prog.fence()
        fa.reset()
        ba.reset()

        xtp = [fa.buf(2048, f"xt2{i}") for i in range(2)]
        xnp = [fa.buf(2048, f"xn2{i}") for i in range(2)]
        lndump = fa.buf(2048, "lndump2")
        rtmp = fa.buf(1024, "rtmp2")
        qp = fa.buf(1024, "qp")
        sqfp = [fa.buf(1024, f"sqf{i}") for i in range(2)]
        qtfp = [fa.buf(1024, f"qtf{i}") for i in range(2)]
        elp = [(fa.buf(512, f"el0{i}"), fa.buf(512, f"el1{i}")) for i in range(2)]
        wq_hi = ba.buf(16 * 1024, "wq_hi")
        wq_lo = ba.buf(16 * 1024, "wq_lo")
        xnThp = [ba.buf(2048, f"xnTh2{i}") for i in range(2)]
        xnTlp = [ba.buf(2048, f"xnTl2{i}") for i in range(2)]
        bq = [ba.buf(512, f"bq{i}") for i in range(4)]
        bq2 = [ba.buf(512, f"bq2{i}") for i in range(4)]

        for half in range(2):
            hof = half * 1024
            for dc in range(16):
                prog.dma("sync", lndump.ap[:, 0:1024],
                         wq[dc * 128:(dc + 1) * 128, hof:hof + 1024],
                         writes=(lndump,))
                prog.op("vector", lambda e, s=lndump.ap[:, 0:1024],
                        h=wq_hi.ap[:, dc * 1024:(dc + 1) * 1024]:
                        e.tensor_copy(out=h, in_=s),
                        reads=(lndump,), writes=(wq_hi,))
                prog.op("vector", lambda e, s=lndump.ap[:, 0:1024],
                        h=wq_hi.ap[:, dc * 1024:(dc + 1) * 1024],
                        l=wq_lo.ap[:, dc * 1024:(dc + 1) * 1024]:
                        e.tensor_tensor(out=l, in0=s, in1=h, op=ALU.subtract),
                        reads=(lndump, wq_hi), writes=(wq_lo,))

            for idx in range(NOWN):
                par = idx % 2
                t0 = idx * 128
                xt, xn = xtp[par], xnp[par]
                sqf, qtf = sqfp[par], qtfp[par]
                el0, el1 = elp[par]
                xnT_hi, xnT_lo = xnThp[par], xnTlp[par]
                st = statp[par]
                rb = ropep[par]
                prog.dma("sync", xt.ap, xown[t0:t0 + 128, :], writes=(xt,))
                for j, nm in enumerate(("ocosq_hi", "ocosq_lo", "osinq_hi",
                                        "osinq_lo")):
                    prog.dma("sync", rb.ap[:, j * 64:(j + 1) * 64],
                             oropes[nm][t0:t0 + 128, :], writes=(rb,))
                prog.dma("sync", el0.ap,
                         expLam[(2 * idx) * 128:(2 * idx + 1) * 128, :],
                         writes=(el0,))
                prog.dma("sync", el1.ap,
                         expLam[(2 * idx + 1) * 128:(2 * idx + 2) * 128, :],
                         writes=(el1,))
                ln_token(xt, xn, lndump, 2048, EPS1, 0, st=st)
                for rnd in range(4):
                    bank = pb[rnd % 2]
                    transpose_round(xn, [(rnd * 4 + j) * 128 for j in range(4)],
                                    bank)
                    prog.op("vector", lambda e, b=bank.ap,
                            h=xnT_hi.ap[:, rnd * 512:(rnd + 1) * 512]:
                            e.tensor_copy(out=h, in_=b),
                            reads=(bank,), writes=(xnT_hi,))
                    prog.op("vector", lambda e, b=bank.ap,
                            h=xnT_hi.ap[:, rnd * 512:(rnd + 1) * 512],
                            l=xnT_lo.ap[:, rnd * 512:(rnd + 1) * 512]:
                            e.tensor_tensor(out=l, in0=b, in1=h,
                                            op=ALU.subtract),
                            reads=(bank, xnT_hi), writes=(xnTlp[par],))
                for jt in range(2):
                    bank = pb[2 + jt + 2 * par]
                    nmm = 0
                    for xs, ws in ((xnT_hi, wq_hi), (xnT_lo, wq_hi),
                                   (xnT_hi, wq_lo)):
                        for dc in range(16):
                            nmm += 1
                            mm(bank, xs, xs.ap[:, dc * 128:(dc + 1) * 128],
                               ws, ws.ap[:, dc * 1024 + jt * 512:
                                         dc * 1024 + jt * 512 + 512],
                               start=(nmm == 1), stop=(nmm == 48))
                qb0, qb1 = pb[2 + 2 * par], pb[3 + 2 * par]
                prog.op("vector", lambda e, o=qp.ap, a=qb0.ap, b=qb1.ap:
                        e.tensor_copy(out=o[:, 0:512], in_=a),
                        reads=(qb0,), writes=(qp,))
                prog.op("vector", lambda e, o=qp.ap, b=qb1.ap:
                        e.tensor_copy(out=o[:, 512:1024], in_=b),
                        reads=(qb1,), writes=(qp,))

                emit_rope(qp, sqf, rtmp, qp.ap, sqf.ap, rtmp.ap, 8, rb=rb)

                prog.op("vector", lambda e, kk=qp.ap, s=st: e.tensor_reduce(
                    out=s.ap[:, 8:16],
                    in_=kk.rearrange("p (h d) -> p h d", h=8),
                    axis=AX.X, op=ALU.max, negate=True),
                    reads=(qp,), writes=(st,))
                for hh in range(8):
                    prog.op("scalar", lambda e, kk=qp.ap, oo=qtf.ap, h=hh, s=st:
                            e.activation(out=oo[:, h * 128:(h + 1) * 128],
                                         in_=kk[:, h * 128:(h + 1) * 128],
                                         func=AFT.Exp,
                                         bias=s.ap[:, 8 + h:9 + h],
                                         scale=1.0,
                                         accum_out=s.ap[:, 16 + h:17 + h]),
                            reads=(qp, st), writes=(qtf, st))
                prog.op("vector", lambda e, s=st: e.reciprocal(
                    out=s.ap[:, 24:32], in_=s.ap[:, 16:24]),
                    reads=(st,), writes=(st,))
                for hh in range(8):
                    prog.op("vector", lambda e, oo=qtf.ap, h=hh, s=st:
                            e.tensor_single_scalar(
                                out=oo[:, h * 128:(h + 1) * 128],
                                in_=oo[:, h * 128:(h + 1) * 128],
                                scalar=s.ap[:, 24 + h:25 + h], op=ALU.mult),
                            reads=(qtf, st), writes=(qtf,))
                eof = half * 256
                prog.op("vector", lambda e, eo=eof, e0=el0.ap, t=rtmp.ap:
                        e.tensor_single_scalar(
                            out=t[:, 0:256], in_=e0[:, eo:eo + 256],
                            scalar=wsel.ap[:, 0:1], op=ALU.mult),
                        reads=(el0, wsel), writes=(rtmp,))
                prog.op("vector", lambda e, eo=eof, e1=el1.ap, t=rtmp.ap:
                        e.scalar_tensor_tensor(
                            out=t[:, 0:256], in0=e1[:, eo:eo + 256],
                            scalar=wsel.ap[:, 1:2], in1=t[:, 0:256],
                            op0=ALU.mult, op1=ALU.add),
                        reads=(el1, wsel, rtmp), writes=(rtmp,))
                prog.op("vector", lambda e, oo=qtf.ap, t=rtmp.ap:
                        e.tensor_tensor(
                            out=oo.rearrange("p (g r d) -> p g r d", g=2, r=4),
                            in0=oo.rearrange("p (g r d) -> p g r d", g=2, r=4),
                            in1=t[:, 0:256].rearrange("p (g d) -> p g d", g=2)
                            .unsqueeze(2).broadcast_to((128, 2, 4, 128)),
                            op=ALU.mult),
                        reads=(qtf, rtmp), writes=(qtf,))

                for src_, bufs, dsts in ((sqf, bq, (sqT_hi, sqT_lo)),
                                         (qtf, bq2, (qtT_hi, qtT_lo))):
                    for rnd in range(2):
                        bank = pb[6 + rnd % 2]
                        transpose_round(src_,
                                        [(rnd * 4 + j) * 128 for j in range(4)],
                                        bank)
                        split(bank, bufs[2 * par], bufs[2 * par + 1])
                        for z, dst in enumerate(dsts):
                            prog.dma("gpsimd",
                                     dst[:, t0 * 16 + half * 1024 + rnd * 512:
                                         t0 * 16 + half * 1024 + (rnd + 1) * 512],
                                     bufs[2 * par + z].ap,
                                     reads=(bufs[2 * par + z],))
        prog.fence()
        fa.reset()
        ba.reset()

        ogla = fa.buf(16 * 512, "ogla")
        Sst = fa.buf(128, "Sst")
        Ssc = fa.buf(128, "Ssc")
        Pf = [fa.buf(512, f"Pf{i}") for i in range(2)]
        osf = fa.buf(512, "osf")
        amf = fa.buf(512, "amf")
        skh = ba.buf(4096, "skh")
        skl = ba.buf(4096, "skl")
        svh = ba.buf(4096, "svh")
        svl = ba.buf(4096, "svl")
        khh = ba.buf(4096, "khh")
        khl = ba.buf(4096, "khl")
        kth = ba.buf(4096, "kth")
        ktl = ba.buf(4096, "ktl")
        qth = [ba.buf(512, f"qth{i}") for i in range(2)]
        qtl = [ba.buf(512, f"qtl{i}") for i in range(2)]
        sqh = [ba.buf(512, f"sqh{i}") for i in range(2)]
        sql = [ba.buf(512, f"sql{i}") for i in range(2)]
        Shi = ba.buf(128, "Shi")
        Slo = ba.buf(128, "Slo")
        Phi = [ba.buf(512, f"Phi{i}") for i in range(2)]
        Plo = [ba.buf(512, f"Plo{i}") for i in range(2)]
        Ahi = ba.buf(512, "Ahi")
        Alo = ba.buf(512, "Alo")
        obh = [ba.buf(512, f"obh{i}") for i in range(2)]
        obl = [ba.buf(512, f"obl{i}") for i in range(2)]

        prog.dma("sync", dct.ap, dCT, reads=(), writes=(dct,))

        for g in range(KV):
            prog.dma("sync", skh.ap, skT_hi[g * 128:(g + 1) * 128, :], writes=(skh,))
            prog.dma("sync", skl.ap, skT_lo[g * 128:(g + 1) * 128, :], writes=(skl,))
            prog.dma("sync", kth.ap, ktT_hi[g * 128:(g + 1) * 128, :], writes=(kth,))
            prog.dma("sync", ktl.ap, ktT_lo[g * 128:(g + 1) * 128, :], writes=(ktl,))
            prog.dma("sync", svh.ap.rearrange("p (c d) -> p c d", c=32),
                     svs_hi.rearrange("(c p) e -> p c e", p=128)
                     [:, :, g * 128:(g + 1) * 128], writes=(svh,))
            prog.dma("sync", svl.ap.rearrange("p (c d) -> p c d", c=32),
                     svs_lo.rearrange("(c p) e -> p c e", p=128)
                     [:, :, g * 128:(g + 1) * 128], writes=(svl,))
            prog.dma("sync", khh.ap.rearrange("p (c d) -> p c d", c=32),
                     khat_hi.rearrange("(c p) e -> p c e", p=128)
                     [:, :, g * 128:(g + 1) * 128], writes=(khh,))
            prog.dma("sync", khl.ap.rearrange("p (c d) -> p c d", c=32),
                     khat_lo.rearrange("(c p) e -> p c e", p=128)
                     [:, :, g * 128:(g + 1) * 128], writes=(khl,))

            for c in range(NB):
                idx = c // 2
                delta = c % 2
                mg = (mg0, mg1)[delta]
                for pi, (l_, r_) in enumerate(((khh, svh), (khl, svh), (khh, svl))):
                    mm(pb[5], l_, l_.ap[:, c * 128:(c + 1) * 128],
                       r_, r_.ap[:, c * 128:(c + 1) * 128],
                       start=(pi == 0), stop=(pi == 2),
                       out_ap=pb[5].ap[:, 0:128])
                if delta == 0:
                    prog.dma("sync", qth[idx % 2].ap,
                             qtT_hi[:, idx * 2048 + g * 512:
                                    idx * 2048 + (g + 1) * 512],
                             writes=(qth[idx % 2],))
                    prog.dma("sync", qtl[idx % 2].ap,
                             qtT_lo[:, idx * 2048 + g * 512:
                                    idx * 2048 + (g + 1) * 512],
                             writes=(qtl[idx % 2],))
                qh, ql = qth[idx % 2], qtl[idx % 2]
                oglab = pb[3 + idx % 2]
                first_mm_done = (delta == 1)
                if c > 0:
                    def ssc(e, dcol=delta):
                        return e.tensor_single_scalar(
                            out=Ssc.ap, in_=Sst.ap,
                            scalar=wsel.ap[:, dcol:dcol + 1], op=ALU.mult)
                    prog.op("vector", ssc, reads=(Sst, wsel), writes=(Ssc,))
                    split(Ssc, Shi, Slo)
                for pi, (l_, r_) in enumerate(((kth, qh), (ktl, qh), (kth, ql))):
                    mm(pb[6], l_, l_.ap[:, c * 128:(c + 1) * 128], r_, r_.ap,
                       start=(pi == 0), stop=(pi == 2))
                if c > 0:
                    for pi, (l_, r_) in enumerate(((Shi, qh), (Slo, qh),
                                                   (Shi, ql))):
                        mm(oglab, l_, l_.ap, r_, r_.ap,
                           start=(delta == 0 and pi == 0), stop=False)
                        first_mm_done = True
                prog.op("vector", lambda e, m=mg.ap: e.tensor_tensor(
                    out=amf.ap, in0=pb[6].ap, in1=m, op=ALU.mult),
                    reads=(pb[6], mg), writes=(amf,))
                split(amf, Ahi, Alo)
                for pi, (l_, r_) in enumerate((
                        (svh, Ahi), (svl, Ahi), (svh, Alo))):
                    mm(oglab, l_, l_.ap[:, c * 128:(c + 1) * 128], r_, r_.ap,
                       start=(c == 0 and pi == 0),
                       stop=(delta == 1 and pi == 2))
                if delta == 1:
                    prog.op("vector", lambda e, o=ogla.ap[:, idx * 512:(idx + 1) * 512],
                            b=oglab.ap: e.tensor_copy(out=o, in_=b),
                            reads=(oglab,), writes=(ogla,))
                if c == 0:
                    prog.op("vector", lambda e: e.tensor_copy(
                        out=Sst.ap, in_=pb[5].ap[:, 0:128]),
                        reads=(pb[5],), writes=(Sst,))
                else:
                    def upd(e, col=c * 4 + g):
                        return e.scalar_tensor_tensor(
                            out=Sst.ap, in0=Sst.ap,
                            scalar=dct.ap[:, col:col + 1],
                            in1=pb[5].ap[:, 0:128], op0=ALU.mult, op1=ALU.add)
                    prog.op("vector", upd, reads=(Sst, dct, pb[5]), writes=(Sst,))

            if "dbg_ogla" in dbg:
                prog.dma("gpsimd", dbg_ogla[g * 128:(g + 1) * 128, :], ogla.ap,
                         reads=(ogla,))
            for idx in range(NOWN):
                qpar = idx % 2
                prog.dma("sync", sqh[qpar].ap,
                         sqT_hi[:, idx * 2048 + g * 512:idx * 2048 + (g + 1) * 512],
                         writes=(sqh[qpar],))
                prog.dma("sync", sql[qpar].ap,
                         sqT_lo[:, idx * 2048 + g * 512:idx * 2048 + (g + 1) * 512],
                         writes=(sql[qpar],))
                J = 2 * idx + 2

                def emit_scores(j):
                    jp = j % 2
                    scb = pb[jp]
                    for pi, (l_, r_) in enumerate((
                            (skh, sqh[qpar]), (skl, sqh[qpar]),
                            (skh, sql[qpar]))):
                        mm(scb, l_, l_.ap[:, j * 128:(j + 1) * 128], r_, r_.ap,
                           start=(pi == 0), stop=(pi == 2))
                    prog.op("scalar", lambda e, o=Pf[jp].ap, b=scb.ap:
                            e.activation(out=o, in_=b, func=AFT.Exp),
                            reads=(scb,), writes=(Pf[jp],))
                    if j >= J - 2:
                        mmask = (md0, md1)[j - (J - 2)]
                        prog.op("vector", lambda e, o=Pf[jp].ap, m=mmask.ap:
                                e.tensor_tensor(out=o, in0=o, in1=m,
                                                op=ALU.mult),
                                reads=(Pf[jp], mmask), writes=(Pf[jp],))
                    split(Pf[jp], Phi[jp], Plo[jp])

                def emit_lpv(j):
                    jp = j % 2
                    for z, Ps in enumerate((Phi[jp], Plo[jp])):
                        mm(pb[7], oc, oc.ap, Ps, Ps.ap,
                           start=(j == 0 and z == 0),
                           stop=(j == J - 1 and z == 1),
                           out_ap=pb[7].ap[0:1, :])
                    for pi, (l_, r_) in enumerate((
                            (svh, Phi[jp]), (svl, Phi[jp]), (svh, Plo[jp]))):
                        mm(pb[2], l_, l_.ap[:, j * 128:(j + 1) * 128], r_, r_.ap,
                           start=(j == 0 and pi == 0),
                           stop=(j == J - 1 and pi == 2))

                for j in range(J):
                    emit_scores(j)
                    if j >= 1:
                        emit_lpv(j - 1)
                emit_lpv(J - 1)
                prog.op("vector", lambda e: e.tensor_copy(
                    out=rowb.ap[0:1, :], in_=pb[7].ap[0:1, :]),
                    reads=(pb[7],), writes=(rowb,))
                prog.op("vector", lambda e: e.reciprocal(
                    out=rowb.ap[0:1, :], in_=rowb.ap[0:1, :]),
                    reads=(rowb,), writes=(rowb,))
                prog.op("vector", lambda e: e.tensor_single_scalar(
                    out=rowb.ap[0:1, :], in_=rowb.ap[0:1, :], scalar=0.5,
                    op=ALU.mult), reads=(rowb,), writes=(rowb,))
                mm(pb[6], orow, orow.ap, rowb, rowb.ap[0:1, :], start=True,
                   stop=True)
                prog.op("vector", lambda e: e.tensor_copy(
                    out=amf.ap, in_=pb[6].ap), reads=(pb[6],), writes=(amf,))
                prog.op("vector", lambda e: e.tensor_tensor(
                    out=osf.ap, in0=pb[2].ap, in1=amf.ap, op=ALU.mult),
                    reads=(pb[2], amf), writes=(osf,))
                prog.op("vector", lambda e,
                        og=ogla.ap[:, idx * 512:(idx + 1) * 512]:
                        e.scalar_tensor_tensor(out=osf.ap, in0=og, scalar=0.5,
                                               in1=osf.ap, op0=ALU.mult,
                                               op1=ALU.add),
                        reads=(ogla, osf), writes=(osf,))
                split(osf, obh[qpar], obl[qpar])
                for z, (bufv, dst) in enumerate(((obh[qpar], oT_hi),
                                                 (obl[qpar], oT_lo))):
                    prog.dma("gpsimd",
                             dst.rearrange("p (h t) -> p h t", h=16)
                             [:, 4 * g:4 * g + 4, idx * 128:(idx + 1) * 128],
                             bufv.ap.rearrange("p (h t) -> p h t", h=4),
                             reads=(bufv,))
        prog.fence()
        fa.reset()
        ba.reset()

        featb = fa.buf(512, "featb")
        xt4 = fa.buf(2048, "xt4")
        xn4 = fa.buf(2048, "xn4")
        gb4 = fa.buf(2048, "gb4")
        wfl = fa.buf(2048, "wfl")
        rem4 = fa.buf(2048, "rem4")
        o_h = ba.buf(16 * 512, "o_h")
        o_l = ba.buf(16 * 512, "o_l")
        wo_h = [ba.buf(2048, f"wo_h{i}") for i in range(2)]
        wo_l = [ba.buf(2048, f"wo_l{i}") for i in range(2)]
        xqb = [ba.buf(512, f"xqb{i}") for i in range(2)]
        atok = ba.f32buf(4 * 2048, "atok")

        for g4 in range(4):
            prog.dma("sync", o_h.ap.rearrange("p (h t) -> p h t", h=16),
                     oT_hi.rearrange("p (h t) -> p h t", h=16)
                     [:, :, g4 * 512:(g4 + 1) * 512], writes=(o_h,))
            prog.dma("sync", o_l.ap.rearrange("p (h t) -> p h t", h=16),
                     oT_lo.rearrange("p (h t) -> p h t", h=16)
                     [:, :, g4 * 512:(g4 + 1) * 512], writes=(o_l,))
            for jc in range(16):
                jp = jc % 2
                prog.dma("sync", wo_h[jp].ap.rearrange("p (a j) -> p a j", a=16),
                         woh.rearrange("(a p) j -> p a j", p=128)
                         [:, :, jc * 128:(jc + 1) * 128], writes=(wo_h[jp],))
                prog.dma("sync", wo_l[jp].ap.rearrange("p (a j) -> p a j", a=16),
                         wol.rearrange("(a p) j -> p a j", p=128)
                         [:, :, jc * 128:(jc + 1) * 128], writes=(wo_l[jp],))
                bank = pb[jp]
                nmm = 0
                for os_, ws_ in ((o_h, wo_h[jp]), (o_l, wo_h[jp]),
                                 (o_h, wo_l[jp])):
                    for dc in range(16):
                        nmm += 1
                        mm(bank, ws_, ws_.ap[:, dc * 128:(dc + 1) * 128],
                           os_, os_.ap[:, dc * 512:(dc + 1) * 512],
                           start=(nmm == 1), stop=(nmm == 48))
                prog.op("vector", lambda e, b=bank.ap: e.tensor_copy(
                    out=featb.ap, in_=b), reads=(bank,), writes=(featb,))
                tbank = pb[2 + jp]
                transpose_round(featb, [j * 128 for j in range(4)], tbank)
                prog.op("vector", lambda e, b=tbank.ap, col=jc * 128:
                        e.tensor_copy(
                            out=atok.ap.rearrange("p (b j) -> p b j", b=4)
                            [:, :, col:col + 128],
                            in_=b.rearrange("p (b j) -> p b j", b=4)),
                        reads=(tbank,), writes=(atok,))

            for tsub in range(4):
                idx = g4 * 4 + tsub
                t0 = idx * 128
                prog.dma("sync", xt4.ap, xown[t0:t0 + 128, :], writes=(xt4,))
                ln_token(atok, gb4, rem4, 2048, EPS1, 0,
                         x_ap=atok.ap[:, tsub * 2048:(tsub + 1) * 2048])
                prog.op("vector", lambda e: e.tensor_tensor(
                    out=xn4.ap, in0=xt4.ap, in1=gb4.ap, op=ALU.add),
                    reads=(xt4, gb4), writes=(xn4,))
                prog.dma("gpsimd", x1s[t0:t0 + 128, :], xn4.ap,
                         reads=(xn4,), writes=(x1s_b,))
        prog.fence()
        fa.reset()
        ba.reset()

        guv_s = fa.buf(512, "guv")
        slv = fa.buf(512, "slv")
        pflo = fa.buf(512, "pflo")
        sinv_t = fa.buf(512, "sinv_t")
        dacc = fa.buf(16 * 512, "dacc")
        ftok = fa.buf(2048, "ftok")
        x1t = fa.buf(2048, "x1t")
        lnb = fa.buf(2048, "lnb")
        sqt5 = lnb
        xq_s = ba.buf(16 * 512, "xq_s")
        p_h = [ba.buf(512, f"p_h{i}") for i in range(16)]
        p_l = [ba.buf(512, f"p_l{i}") for i in range(16)]
        wgs = [ba.buf(2048, f"wgs{i}") for i in range(2)]
        wus = [ba.buf(2048, f"wus{i}") for i in range(2)]
        q_x1 = ba.f32buf(2048, "q_x1")
        q_gb = ba.f32buf(2048, "q_gb")
        q_w = ba.f32buf(2048, "q_w")
        xqT_b = Buf(None, "xqT_dram")

        def emit_prologue(g5):
            for tsub in range(4):
                idx = g5 * 4 + tsub
                t0b = idx * 128
                st5 = statp[tsub % 2]
                prog.dma("sync", q_x1.ap, x1s[t0b:t0b + 128, :],
                         reads=(x1s_b,), writes=(q_x1,))
                ln_token(q_x1, q_gb, q_w, 2048, EPS2, 8, st=st5)
                prog.op("vector", lambda e, s=st5: e.tensor_reduce(
                    out=s.ap[:, 30:31], in_=q_gb.ap, axis=AX.X, op=ALU.max,
                    apply_absolute_value=True), reads=(q_gb,), writes=(st5,))
                prog.op("vector", lambda e, s=st5: e.reciprocal(
                    out=s.ap[:, 31:32], in_=s.ap[:, 30:31]),
                    reads=(st5,), writes=(st5,))
                prog.op("vector", lambda e, s=st5: e.tensor_single_scalar(
                    out=s.ap[:, 31:32], in_=s.ap[:, 31:32], scalar=127.0,
                    op=ALU.mult), reads=(st5,), writes=(st5,))
                prog.op("vector", lambda e, s=st5, col=34 + (g5 % 2) * 8 + tsub:
                        e.tensor_single_scalar(
                            out=stat.ap[:, col:col + 1], in_=s.ap[:, 30:31],
                            scalar=1.0 / 127.0, op=ALU.mult),
                        reads=(st5,), writes=(stat,))
                prog.op("vector", lambda e, s=st5: e.tensor_scalar(
                    out=q_w.ap, in0=q_gb.ap, scalar1=s.ap[:, 31:32],
                    scalar2=128.5, op0=ALU.mult, op1=ALU.add),
                    reads=(q_gb, st5), writes=(q_w,))
                prog.op("vector", lambda e: e.tensor_copy(
                    out=q_x1.ap, in_=q_w.ap), reads=(q_w,), writes=(q_x1,))
                for bbit in (128.0, 64.0, 32.0, 16.0, 8.0, 4.0, 2.0, 1.0):
                    prog.op("vector", lambda e, bb=bbit: e.tensor_scalar(
                        out=q_gb.ap, in0=q_x1.ap, scalar1=bb, scalar2=bb,
                        op0=ALU.is_ge, op1=ALU.mult),
                        reads=(q_x1,), writes=(q_gb,))
                    prog.op("vector", lambda e: e.tensor_tensor(
                        out=q_x1.ap, in0=q_x1.ap, in1=q_gb.ap,
                        op=ALU.subtract), reads=(q_x1, q_gb), writes=(q_x1,))
                prog.op("vector", lambda e: e.scalar_tensor_tensor(
                    out=q_w.ap, in0=q_w.ap, scalar=-128.0, in1=q_x1.ap,
                    op0=ALU.add, op1=ALU.subtract),
                    reads=(q_w, q_x1), writes=(q_w,))
                for rnd in range(4):
                    tbank = pb[6 + rnd % 2]
                    transpose_round(q_w,
                                    [(rnd * 4 + j) * 128 for j in range(4)],
                                    tbank)
                    prog.op("vector", lambda e, b=tbank.ap,
                            q=xqb5[rnd % 2].ap: e.tensor_copy(out=q, in_=b),
                            reads=(tbank,), writes=(xqb5[rnd % 2],))
                    prog.dma("gpsimd",
                             xqT[(rnd * 4) * 128:(rnd * 4 + 4) * 128,
                                 t0b:t0b + 128]
                             .rearrange("(a p) t -> p a t", p=128),
                             xqb5[rnd % 2].ap.rearrange("p (a t) -> p a t", a=4),
                             reads=(xqb5[rnd % 2],), writes=(xqT_b,))

        xqb5 = [ba.buf(512, f"xqb5{i}") for i in range(2)]

        emit_prologue(0)
        for g5 in range(4):
            t0 = g5 * 512
            prog.dma("sync", xq_s.ap.rearrange("p (a t) -> p a t", a=16),
                     xqT.rearrange("(a p) t -> p a t", p=128)[:, :, t0:t0 + 512],
                     reads=(xqT_b,), writes=(xq_s,))
            for ts in range(4):
                col = 34 + (g5 % 2) * 8 + ts
                mm(pb[5], stat, stat.ap[:, col:col + 1], idf, idf.ap,
                   start=True, stop=True,
                   out_ap=pb[5].ap[0:1, ts * 128:(ts + 1) * 128])
            prog.op("vector", lambda e: e.tensor_copy(out=rowb.ap[0:1, :],
                                                      in_=pb[5].ap[0:1, :]),
                    reads=(pb[5],), writes=(rowb,))
            mm(pb[5], orow, orow.ap, rowb, rowb.ap[0:1, :], start=True, stop=True)
            prog.op("vector", lambda e: e.tensor_copy(out=sinv_t.ap,
                                                      in_=pb[5].ap),
                    reads=(pb[5],), writes=(sinv_t,))

            for hh in range(4):
                for hl in range(16):
                    hc = hh * 16 + hl
                    hp = hl % 2
                    prog.dma("sync",
                             wgs[hp].ap.rearrange("p (a h) -> p a h", a=16),
                             wgT[hc * 2048:(hc + 1) * 2048, :]
                             .rearrange("(a p) h -> p a h", p=128),
                             writes=(wgs[hp],))
                    prog.dma("sync",
                             wus[hp].ap.rearrange("p (a h) -> p a h", a=16),
                             wuT[hc * 2048:(hc + 1) * 2048, :]
                             .rearrange("(a p) h -> p a h", p=128),
                             writes=(wus[hp],))
                    for z, (wsb, bank) in enumerate(((wgs[hp], pb[0 + hp]),
                                                     (wus[hp], pb[2 + hp]))):
                        for dc in range(16):
                            mm(bank, wsb, wsb.ap[:, dc * 128:(dc + 1) * 128],
                               xq_s, xq_s.ap[:, dc * 512:(dc + 1) * 512],
                               start=(dc == 0), stop=(dc == 15))
                    gbank, ubank = pb[0 + hp], pb[2 + hp]
                    prog.op("vector", lambda e, gu=guv_s.ap, b=gbank.ap:
                            e.tensor_tensor(out=gu, in0=b, in1=sinv_t.ap,
                                            op=ALU.mult),
                            reads=(gbank, sinv_t), writes=(guv_s,))
                    prog.op("scalar", lambda e, gu=guv_s.ap: e.activation(
                        out=slv.ap, in_=gu, func=AFT.Silu),
                        reads=(guv_s,), writes=(slv,))
                    prog.op("vector", lambda e, b=ubank.ap: e.tensor_tensor(
                        out=pflo.ap, in0=slv.ap, in1=b, op=ALU.mult),
                        reads=(slv, ubank), writes=(pflo,))
                    split(pflo, p_h[hl], p_l[hl])

                for ds in range(16):
                    dp = ds % 2
                    prog.dma("sync",
                             wgs[dp].ap.rearrange("p (a h) -> p a h", a=16),
                             wdT[(ds * 64 + hh * 16) * 128:
                                 (ds * 64 + hh * 16 + 16) * 128, :]
                             .rearrange("(a p) h -> p a h", p=128),
                             writes=(wgs[dp],))
                    bank = pb[4 + dp]
                    nmm = 0
                    for hl in range(16):
                        wcol = hl * 128
                        for Ps in (p_h[hl], p_l[hl]):
                            nmm += 1
                            mm(bank, wgs[dp], wgs[dp].ap[:, wcol:wcol + 128],
                               Ps, Ps.ap, start=(nmm == 1), stop=(nmm == 32))
                    if hh == 0:
                        prog.op("vector", lambda e,
                                o=dacc.ap[:, ds * 512:(ds + 1) * 512],
                                b=bank.ap: e.tensor_copy(out=o, in_=b),
                                reads=(bank,), writes=(dacc,))
                    else:
                        prog.op("vector", lambda e,
                                o=dacc.ap[:, ds * 512:(ds + 1) * 512],
                                b=bank.ap: e.tensor_tensor(out=o, in0=o, in1=b,
                                                           op=ALU.add),
                                reads=(bank, dacc), writes=(dacc,))
                if hh == 0 and g5 < 3:
                    emit_prologue(g5 + 1)

            for tsub in range(4):
                idx = g5 * 4 + tsub
                t0b = idx * 128
                for rnd in range(4):
                    tbank = pb[6 + rnd % 2]
                    transpose_round(dacc,
                                    [(rnd * 4 + j) * 512 + tsub * 128
                                     for j in range(4)], tbank)
                    prog.op("vector", lambda e, b=tbank.ap, r=rnd:
                            e.tensor_copy(out=ftok.ap[:, r * 512:(r + 1) * 512],
                                          in_=b),
                            reads=(tbank,), writes=(ftok,))
                prog.dma("sync", x1t.ap, x1s[t0b:t0b + 128, :],
                         reads=(x1s_b,), writes=(x1t,))
                ln_token(ftok, lnb, sqt5, 2048, EPS1, 0, st=statp[tsub % 2])
                prog.op("vector", lambda e: e.tensor_tensor(
                    out=x1t.ap, in0=x1t.ap, in1=lnb.ap, op=ALU.add),
                    reads=(x1t, lnb), writes=(x1t,))
                prog.dma("gpsimd", out[t0b:t0b + 128, :], x1t.ap,
                         reads=(x1t,))

        last_stores = [o.oid for o in prog.ops if o.is_dma and o.engine == "gpsimd"]
        prog.op("gpsimd", lambda e: None, extra_deps=(last_stores[-1],))

        block = es.enter_context(nc.Block())
        prog.emit(nc, es, block)

    return nc


_CONSTS = None


def _shard_inputs(inputs):
    global _CONSTS
    if _CONSTS is None:
        _CONSTS = _host_constants()
    consts = _CONSTS
    x = np.ascontiguousarray(np.asarray(inputs["L0_attn_hidden_states"],
                                        dtype=np.float32))
    w = {k: np.ascontiguousarray(np.asarray(inputs[k2], dtype=np.float32))
         for k, k2 in (("wq", "L0_attn_Wq"), ("wk", "L0_attn_Wk"),
                       ("wv", "L0_attn_Wv"), ("wo", "L0_attn_Wo"),
                       ("wg", "L0_ffn_W_g"), ("wu", "L0_ffn_W_u"),
                       ("wd", "L0_ffn_W_d"))}
    shared = {k: consts[k] for k in ("ident_f32", "ident_bf16", "lmat", "ones_col_bf16",
                                     "ones_row_f32", "cosk_hi", "cosk_lo",
                                     "sink_hi", "sink_lo")}
    par_in = [_parity_inputs(0, consts), _parity_inputs(1, consts)]
    maps = []
    for c in range(8):
        b, par = c // 2, c % 2
        pi = par_in[par]
        m = dict(xb=np.ascontiguousarray(x[b]),
                 xown=np.ascontiguousarray(x[b][pi["_idxs"]]))
        m.update(w)
        m.update(shared)
        m.update({k: v for k, v in pi.items() if not k.startswith("_")})
        maps.append(m)
    return maps


def kernel(**inputs):
    maps = _shard_inputs(inputs)
    nc = build_program()
    res = run_bass_kernel_spmd(nc, maps, core_ids=list(range(8)))
    outp = np.zeros((B, N, D), dtype=np.float32)
    par_in = [_parity_inputs(0, _CONSTS), _parity_inputs(1, _CONSTS)]
    for c in range(8):
        b, par = c // 2, c % 2
        outp[b].reshape(-1, D)[par_in[par]["_idxs"]] = res.results[c]["out"]
    return outp
